# version 03
# baseline (speedup 1.0000x reference)
import numpy as np

import concourse.bass as bass
import concourse.mybir as mybir
from concourse.bass_utils import run_bass_kernel_spmd

F32 = mybir.dt.float32
BF16 = mybir.dt.bfloat16
AF = mybir.ActivationFunctionType
ALU = mybir.AluOpType
AXX = mybir.AxisListType.X

B = 4
T = 4096
D = 2048
NH = 16
HC = 16
INTER = 5632
IC = 44
OWN = 2048
NBLK = 4
NSLICE = 8
SHIFT = 24.0
EPS = 1e-5
INVSQ = float(1.0 / np.sqrt(128.0))
DOWN_G = 6

ENGS = ("sync", "scalar", "vector", "tensor", "gpsimd")

DEBUG = False


def nkc(j):
    return 8 * (j + 1)


class Sem:
    def __init__(self, name):
        self.name = name
        self.n = 0
        self.h = None


class Prog:
    def __init__(self):
        self.ops = {e: [] for e in ENGS}

    def op(self, eng, fn, waits=(), incs=(), tag=""):
        for s, amt in incs:
            s.n += amt
        self.ops[eng].append((fn, tuple(waits), tuple(incs), tag))

    def check(self):
        vals = {}
        ptr = {e: 0 for e in ENGS}
        progress = True
        while progress:
            progress = False
            for e in ENGS:
                while ptr[e] < len(self.ops[e]):
                    fn, waits, incs, tag = self.ops[e][ptr[e]]
                    if all(vals.get(s.name, 0) >= v for s, v in waits):
                        for s, amt in incs:
                            vals[s.name] = vals.get(s.name, 0) + amt
                        ptr[e] += 1
                        progress = True
                    else:
                        break
        stuck = {e: ptr[e] for e in ENGS if ptr[e] < len(self.ops[e])}
        if stuck:
            lines = []
            for e, i in stuck.items():
                fn, waits, incs, tag = self.ops[e][i]
                wd = [(s.name, v, vals.get(s.name, 0)) for s, v in waits]
                lines.append(f"{e}[{i}/{len(self.ops[e])}] tag={tag} waits={wd}")
            raise RuntimeError("schedule deadlock:\n" + "\n".join(lines))


def build_program():
    nc = bass.Bass("TRN2", target_bir_lowering=False, debug=False,
                   detect_race_conditions=False)

    d_xkv = nc.dram_tensor("xkv", [T, D], F32, kind="ExternalInput").ap()
    d_xq = nc.dram_tensor("xq", [OWN, D], F32, kind="ExternalInput").ap()
    d_wk = nc.dram_tensor("wk", [128, HC * HC * 128], F32, kind="ExternalInput").ap()
    d_wq = nc.dram_tensor("wq", [128, HC * HC * 128], F32, kind="ExternalInput").ap()
    d_wv = nc.dram_tensor("wv", [128, HC * D], F32, kind="ExternalInput").ap()
    d_wo = nc.dram_tensor("wo", [128, HC * D], F32, kind="ExternalInput").ap()
    d_gate = nc.dram_tensor("gate", [IC, 128, HC * 128], F32, kind="ExternalInput").ap()
    d_up = nc.dram_tensor("up", [IC, 128, HC * 128], F32, kind="ExternalInput").ap()
    d_down = nc.dram_tensor("down", [IC, 128, D], F32, kind="ExternalInput").ap()
    d_gs = nc.dram_tensor("gs", [128, IC], F32, kind="ExternalInput").ap()
    d_us = nc.dram_tensor("us", [128, IC], F32, kind="ExternalInput").ap()
    d_ds = nc.dram_tensor("ds", [128, D], F32, kind="ExternalInput").ap()
    d_mask = nc.dram_tensor("maskin", [128, 8 * 512], F32, kind="ExternalInput").ap()
    d_ident = nc.dram_tensor("identin", [128, 128], F32, kind="ExternalInput").ap()
    d_out = nc.dram_tensor("out", [OWN, D], F32, kind="ExternalOutput").ap()

    s_kt = nc.dram_tensor("s_kt", [NH, 128, T], BF16, kind="Internal").ap()
    s_v2 = nc.dram_tensor("s_v2", [NH, T, 128], BF16, kind="Internal").ap()
    s_qt = nc.dram_tensor("s_qt", [NBLK, NH, 128, 512], BF16, kind="Internal").ap()
    s_r1 = nc.dram_tensor("s_r1", [NBLK, 4, 128, D], BF16, kind="Internal").ap()

    base = (nc.sbuf_base + 511) & ~511
    TOP = nc.sbuf_top

    def sb(name, cols_bytes_off, shape, dtype):
        per = int(np.prod(shape[1:])) * mybir.dt.size(dtype)
        assert shape[0] == 128
        assert base + cols_bytes_off + per <= TOP, (name, cols_bytes_off, per)
        return nc.alloc_sbuf_tensor_at(name, list(shape), dtype,
                                       offset=base + cols_bytes_off)

    t_ident = sb("ident", 0x0000, [128, 128], BF16)
    t_mask = sb("mask8", 0x0200, [128, 8 * 512], BF16)
    t_gs = sb("gsb", 0x2200, [128, IC], F32)
    t_us = sb("usb", 0x2300, [128, IC], F32)
    t_ds = sb("dsb", 0x2400, [128, D], F32)
    t_bias = sb("biasc", 0x4400, [128, 4], F32)
    t_ones = sb("ones", 0x4420, [128, 128], BF16)
    t_stat = sb("stats", 0x4520, [128, 24], F32)
    CB = 0x4800

    p1_wk = sb("p1_wk", CB + 0x00000, [128, HC * HC * 128], BF16)
    p1_wv = sb("p1_wv", CB + 0x10000, [128, HC * D], BF16)
    p1_lnT = sb("p1_lnT", CB + 0x20000, [128, HC * 512], BF16)
    p1_x = sb("p1_x", CB + 0x24000, [128, 2 * D], F32)
    p1_ln = sb("p1_ln", CB + 0x28000, [128, 2 * D], BF16)
    p1_kst = sb("p1_kst", CB + 0x2A000, [128, HC * 512], BF16)
    p1_vst = sb("p1_vst", CB + 0x2E000, [128, D], BF16)
    p1b_wq = sb("p1b_wq", CB + 0x10000, [128, HC * HC * 128], BF16)
    p1b_lnT = sb("p1b_lnT", CB + 0x20000, [128, HC * 512], BF16)
    p1b_x = sb("p1b_x", CB + 0x24000, [128, 2 * D], F32)
    p1b_ln = sb("p1b_ln", CB + 0x28000, [128, 2 * D], BF16)
    p1b_qst = sb("p1b_qst", CB + 0x2A000, [128, HC * 512], BF16)
    p2_wo = sb("p2_wo", CB + 0x00000, [128, HC * D], BF16)
    p2_qt = sb("p2_qt", CB + 0x10000, [128, NH * 512], BF16)
    p2_kt = sb("p2_kt", CB + 0x14000, [128, 2 * T], BF16)
    p2_v2 = sb("p2_v2", CB + 0x18000, [128, 2 * T], BF16)
    p2_pt = sb("p2_pt", CB + 0x1C000, [128, 2 * 1024], BF16)
    p2_z = sb("p2_z", CB + 0x1D000, [128, NH * 512], BF16)
    p2_xq = sb("p2_xq", CB + 0x21000, [128, 4 * D], F32)
    p2_il = sb("p2_il", CB + 0x29000, [128, 512], F32)
    p2_r1 = sb("p2_r1", CB + 0x29800, [128, 2 * D], BF16)
    p3_acc = sb("p3_acc", CB + 0x00000, [128, 4 * D], F32)
    p3_lnT = sb("p3_lnT", CB + 0x00000, [128, HC * 512], BF16)
    p3_hdn = sb("p3_hdn", CB + 0x08000, [128, IC * 512], BF16)
    p3_gu = sb("p3_gu", CB + 0x13000, [128, 4 * HC * 128], BF16)
    p3_dn = sb("p3_dn", CB + 0x17000, [128, 2 * DOWN_G * D], BF16)
    p3_r1 = sb("p3_r1", CB + 0x23000, [128, 4 * D], BF16)
    p3_ost = sb("p3_ost", CB + 0x27000, [128, 2 * D], F32)
    p3_gub = sb("p3_gub", CB + 0x2B000, [128, 2 * 512], BF16)

    psA = nc.alloc_psum_tensor("psA", [128, 2048], F32)
    psB = nc.alloc_psum_tensor("psB", [128, 1024], F32)
    psC = nc.alloc_psum_tensor("psC", [128, 512], F32)
    psT = nc.alloc_psum_tensor("psT", [128, 1024], BF16)

    sems = [Sem(n) for n in ("dS", "dG", "eT", "eV", "eS")]
    dS, dG, eT, eV, eS = sems
    P = Prog()

    def dma(q, out, in_, waits=(), tag=""):
        s = dS if q == "sync" else dG
        P.op(q, lambda e, out=out, in_=in_: e.dma_start(out=out, in_=in_),
             waits=waits, incs=[(s, 16)], tag=tag)
        return s.n

    def ten(fn, waits=(), inc=False, tag=""):
        P.op("tensor", fn, waits=waits, incs=[(eT, 1)] if inc else [], tag=tag)
        return eT.n

    def vec(fn, waits=(), tag=""):
        P.op("vector", fn, waits=waits, incs=[(eV, 1)], tag=tag)
        return eV.n

    def act(fn, waits=(), tag=""):
        P.op("scalar", fn, waits=waits, incs=[(eS, 1)], tag=tag)
        return eS.n

    dbg_taps = []

    def dbg_snap(name, ap):
        if not DEBUG:
            return
        dd = nc.dram_tensor(name, list(ap.shape), ap.dtype,
                            kind="ExternalOutput").ap()
        dma("sync", dd, ap,
            waits=[(eV, eV.n), (eS, eS.n), (eT, eT.n), (dG, dG.n)],
            tag=f"snap_{name}")
        dbg_taps.append(name)

    c_ident = dma("gpsimd", t_ident[:], d_ident, tag="ld_ident")
    c_mask = dma("gpsimd", t_mask[:], d_mask, tag="ld_mask")
    dma("sync", t_gs[:], d_gs, tag="ld_gs")
    dma("sync", t_us[:], d_us, tag="ld_us")
    c_ds = dma("sync", t_ds[:], d_ds, tag="ld_ds")
    P.op("gpsimd", lambda e: e.memset(t_bias[:], -SHIFT), tag="set_bias")
    P.op("gpsimd", lambda e: e.memset(t_ones[:], 1.0), incs=[(eV, 1)],
         tag="set_ones")
    c_const = eV.n

    def emit_ln(x_ap, ln_ap, par, waits_x, waits_ln, sq_discard, tag=""):
        bcol = 12 * par

        def col(i):
            return t_stat[:, bcol + i:bcol + i + 1]

        vec(lambda e, x_ap=x_ap: e.reduce_sum(col(0), x_ap, axis=AXX),
            waits=list(waits_x), tag=tag + "_red")
        a_sq = act(lambda e, x_ap=x_ap, sq_discard=sq_discard: e.activation(
            out=sq_discard, in_=x_ap, func=AF.Square),
            waits=list(waits_x) + list(waits_ln), tag=tag + "_sq")
        vec(lambda e, sq_discard=sq_discard: e.reduce_sum(
            col(1), sq_discard, axis=AXX),
            waits=[(eS, a_sq)], tag=tag + "_sqred")
        vec(lambda e: e.tensor_scalar(col(2), col(0), 1.0 / D, None, ALU.mult),
            tag=tag + "_mean")
        vec(lambda e: e.tensor_scalar(col(3), col(1), 1.0 / D, None, ALU.mult),
            tag=tag + "_ex2")
        vec(lambda e: e.tensor_mul(col(4), col(2), col(2)), tag=tag + "_m2")
        vec(lambda e: e.scalar_tensor_tensor(col(5), col(4), -1.0, col(3),
                                             ALU.mult, ALU.add), tag=tag + "_var")
        vec(lambda e: e.tensor_scalar(col(6), col(5), EPS, None, ALU.add),
            tag=tag + "_veps")
        c_rv = vec(lambda e: e.reciprocal(col(7), col(6)), tag=tag + "_rv")
        a_rstd = act(lambda e: e.activation(out=col(8), in_=col(7), func=AF.Sqrt),
                     waits=[(eV, c_rv)], tag=tag + "_rstd")
        c_norm = vec(lambda e, x_ap=x_ap, ln_ap=ln_ap: e.tensor_scalar(
            ln_ap, x_ap, col(2), col(8), ALU.subtract, ALU.mult),
            waits=[(eS, a_rstd)] + list(waits_ln), tag=tag + "_norm")
        return a_sq, c_norm

    def emit_transpose(ln_ap, lnT_h, tt, c_norm, extra_first=(), tag=""):
        last_ev = eV.n
        last_tr = eT.n
        for grp in range(4):
            for q in range(4):
                hc = grp * 4 + q
                w = []
                if grp == 0 and q == 0:
                    w = [(eV, c_norm), (dG, c_ident)] + list(extra_first)
                if q == 0 and grp > 0:
                    w.append((eV, last_ev))
                last_tr = ten(lambda e, hc=hc, q=q, ln_ap=ln_ap: e.transpose(
                    psT[:, q * 128:(q + 1) * 128],
                    ln_ap[:, hc * 128:(hc + 1) * 128], t_ident[:]),
                    waits=w, inc=(q == 3), tag=f"{tag}_tr{grp}")
            last_ev = vec(lambda e, grp=grp, tt=tt, lnT_h=lnT_h: e.tensor_copy(
                lnT_h[:].rearrange("p (hc f) -> p hc f", f=512)
                [:, grp * 4:(grp + 1) * 4, tt * 128:tt * 128 + 128],
                psT[:, 0:512].rearrange("p (q f) -> p q f", f=128)),
                waits=[(eT, last_tr)], tag=f"{tag}_ev{grp}")
        return last_ev, last_tr

    c_wk = dma("gpsimd", p1_wk[:], d_wk, tag="ld_wk")
    c_wv = dma("gpsimd", p1_wv[:], d_wv, tag="ld_wv")

    xsl = lambda b, slot: b[:, slot * D:(slot + 1) * D]
    c_xfree = {}
    c_lnfree = {0: 0, 1: 0}
    c_kev = {0: 0, 1: 0}
    c_kst_dma = 0
    c_vst_dma = 0
    c_vevac_prev = 0
    for s in range(NSLICE):
        c_lnT_ready = 0
        for tt in range(4):
            slot = tt % 2
            wx = []
            if slot in c_xfree:
                wx = [(eS, c_xfree[slot][0]), (eV, c_xfree[slot][1])]
            t0 = s * 512 + tt * 128
            c_x = dma("sync", xsl(p1_x, slot), d_xkv[t0:t0 + 128, :], waits=wx,
                      tag=f"p1a_x_{s}_{tt}")
            a_sq, c_norm = emit_ln(
                xsl(p1_x, slot), xsl(p1_ln, slot), slot,
                waits_x=[(dS, c_x)], waits_ln=[(eT, c_lnfree[slot])],
                sq_discard=xsl(p1_ln, slot), tag=f"p1a_ln_{s}_{tt}")
            c_xfree[slot] = (a_sq, c_norm)
            c_lnT_ready, c_tr = emit_transpose(
                xsl(p1_ln, slot), p1_lnT, tt, c_norm, tag=f"p1a_{s}_{tt}")
            c_lnfree[slot] = c_tr
        for dc in range(NH):
            par = dc % 2
            pk = psB[:, par * 512:(par + 1) * 512]
            for hc in range(HC):
                w = []
                if hc == 0:
                    w = [(eV, c_lnT_ready), (dG, c_wk), (eV, c_kev[par])]
                    if dc < 2:
                        w.append((dS, c_kst_dma))
                ten(lambda e, dc=dc, hc=hc, pk=pk: e.matmul(
                    pk, p1_wk[:, (dc * HC + hc) * 128:(dc * HC + hc + 1) * 128],
                    p1_lnT[:, hc * 512:(hc + 1) * 512],
                    start=(hc == 0), stop=(hc == HC - 1)),
                    waits=w, inc=(hc == HC - 1), tag=f"p1a_kmm_{s}_{dc}")
            cmm = eT.n
            c_kev[par] = vec(lambda e, dc=dc, pk=pk: e.tensor_copy(
                p1_kst[:, dc * 512:(dc + 1) * 512], pk),
                waits=[(eT, cmm)], tag=f"p1a_kev_{s}_{dc}")
        c_kst_dma = dma(
            "sync",
            s_kt.rearrange("h p k -> p h k")[:, :, s * 512:(s + 1) * 512],
            p1_kst[:].rearrange("p (h f) -> p h f", f=512),
            waits=[(eV, c_kev[0]), (eV, c_kev[1])], tag=f"p1a_kspill_{s}")
        for tt in range(4):
            for hc in range(HC):
                for dq in range(4):
                    w = []
                    if hc == 0 and dq == 0:
                        w = [(eV, c_lnT_ready), (dG, c_wv), (eS, c_vevac_prev)]
                    ten(lambda e, hc=hc, tt=tt, dq=dq: e.matmul(
                        psA[:, dq * 512:(dq + 1) * 512],
                        p1_lnT[:, hc * 512 + tt * 128:hc * 512 + tt * 128 + 128],
                        p1_wv[:, hc * D + dq * 512:hc * D + (dq + 1) * 512],
                        start=(hc == 0), stop=(hc == HC - 1)),
                        waits=w, inc=(hc == HC - 1 and dq == 3),
                        tag=f"p1a_vmm_{s}_{tt}")
            cmm = eT.n
            c_vevac_prev = act(lambda e: e.activation(
                out=p1_vst[:], in_=psA[:], func=AF.Copy),
                waits=[(eT, cmm), (dS, c_vst_dma)], tag=f"p1a_vev_{s}_{tt}")
            t0 = s * 512 + tt * 128
            c_vst_dma = dma(
                "sync",
                s_v2.rearrange("h t f -> t h f")[t0:t0 + 128, :, :],
                p1_vst[:].rearrange("p (h f) -> p h f", f=128),
                waits=[(eS, c_vevac_prev)], tag=f"p1a_vspill_{s}_{tt}")
        if s == 0:
            dbg_snap("dbg_p1x", p1_x[:])
            dbg_snap("dbg_p1ln", p1_ln[:])
            dbg_snap("dbg_p1lnT", p1_lnT[:])
            dbg_snap("dbg_p1wk", p1_wk[:])
            dbg_snap("dbg_p1wv", p1_wv[:])
            dbg_snap("dbg_p1kst", p1_kst[:])
            dbg_snap("dbg_p1vst", p1_vst[:])
            dbg_snap("dbg_stat0", t_stat[:])

    c_p1a_kdma = c_kst_dma
    c_p1a_vdma = c_vst_dma
    c_p1a_T = eT.n
    c_p1a_S = eS.n
    c_p1a_V = eV.n

    c_wo = dma("gpsimd", p2_wo[:], d_wo, waits=[(eT, c_p1a_T)], tag="ld_wo")
    c_wq = dma("gpsimd", p1b_wq[:], d_wq, waits=[(eT, c_p1a_T)], tag="ld_wq")

    c_xfree = {}
    c_lnfree = {0: c_p1a_T, 1: c_p1a_T}
    c_qev = {0: c_p1a_V, 1: c_p1a_V}
    c_qst_dma = c_p1a_kdma
    for j in range(NBLK):
        c_lnT_ready = 0
        for tt in range(4):
            slot = tt % 2
            wx = [(eS, c_p1a_S), (eV, c_p1a_V), (eT, c_p1a_T)] if j == 0 and tt < 2 \
                else [(eS, c_xfree[slot][0]), (eV, c_xfree[slot][1])]
            t0 = j * 512 + tt * 128
            c_x = dma("sync", xsl(p1b_x, slot), d_xq[t0:t0 + 128, :], waits=wx,
                      tag=f"p1b_x_{j}_{tt}")
            a_sq, c_norm = emit_ln(
                xsl(p1b_x, slot), xsl(p1b_ln, slot), slot,
                waits_x=[(dS, c_x)], waits_ln=[(eT, c_lnfree[slot])],
                sq_discard=xsl(p1b_ln, slot), tag=f"p1b_ln_{j}_{tt}")
            c_xfree[slot] = (a_sq, c_norm)
            c_lnT_ready, c_tr = emit_transpose(
                xsl(p1b_ln, slot), p1b_lnT, tt, c_norm, tag=f"p1b_{j}_{tt}")
            c_lnfree[slot] = c_tr
        for dc in range(NH):
            par = dc % 2
            pq = psB[:, par * 512:(par + 1) * 512]
            for hc in range(HC):
                w = []
                if hc == 0:
                    w = [(eV, c_lnT_ready), (dG, c_wq), (eV, c_qev[par])]
                    if dc < 2:
                        w.append((dS, c_qst_dma))
                ten(lambda e, dc=dc, hc=hc, pq=pq: e.matmul(
                    pq, p1b_wq[:, (dc * HC + hc) * 128:(dc * HC + hc + 1) * 128],
                    p1b_lnT[:, hc * 512:(hc + 1) * 512],
                    start=(hc == 0), stop=(hc == HC - 1)),
                    waits=w, inc=(hc == HC - 1), tag=f"p1b_qmm_{j}_{dc}")
            cmm = eT.n
            c_qev[par] = vec(lambda e, dc=dc, pq=pq: e.tensor_copy(
                p1b_qst[:, dc * 512:(dc + 1) * 512], pq),
                waits=[(eT, cmm)], tag=f"p1b_qev_{j}_{dc}")
        c_qst_dma = dma(
            "sync",
            s_qt[j].rearrange("h p f -> p h f"),
            p1b_qst[:].rearrange("p (h f) -> p h f", f=512),
            waits=[(eV, c_qev[0]), (eV, c_qev[1])], tag=f"p1b_qspill_{j}")

    c_p1b_qdma = c_qst_dma
    c_p1b_T = eT.n

    ktsl = lambda slot: p2_kt[:, slot * T:(slot + 1) * T]
    v2sl = lambda slot: p2_v2[:, slot * T:(slot + 1) * T]
    ptsl = lambda slot: p2_pt[:, slot * 1024:(slot + 1) * 1024]
    ps_s = lambda par: psA[:, par * 1024:(par + 1) * 1024]
    ps_l = psB[:, 0:512]
    ps_zz = psC

    c_s_free = {0: 0, 1: 0}
    c_pt_free = {0: c_p1b_T, 1: c_p1b_T}
    c_kv_free = {0: c_p1b_T, 1: c_p1b_T}
    c_lz_free = c_const
    c_wo_ev = c_const
    c_r1_dma = {}
    c_xq_free = {qs: (0, 0) for qs in range(4)}
    c_exp_lastpair = [0, 0]

    for j in range(NBLK):
        nk = nkc(j)
        ng = nk // 2
        kmax = nk * 128
        c_qt = dma("sync", p2_qt[:].rearrange("p (h f) -> p h f", f=512),
                   s_qt[j].rearrange("h p f -> p h f"),
                   waits=[(dS, c_p1b_qdma), (eT, c_p1b_T)], tag=f"p2_qt_{j}")
        c_xq = []
        for qs in range(4):
            t0 = j * 512 + qs * 128
            w = [(eT, c_p1b_T)] if j == 0 else [(eV, c_xq_free[qs][0])]
            c_xq.append(dma("sync", p2_xq[:, qs * D:(qs + 1) * D],
                            d_xq[t0:t0 + 128, :], waits=w,
                            tag=f"p2_xq_{j}_{qs}"))

        for h in range(NH):
            slot = h % 2
            c_k = dma("gpsimd", ktsl(slot)[:, 0:kmax], s_kt[h][:, 0:kmax],
                      waits=[(dS, c_p1a_kdma), (eT, c_kv_free[slot])],
                      tag=f"p2_kld_{j}_{h}")
            c_v = dma("gpsimd",
                      v2sl(slot)[:, 0:kmax].rearrange("p (c f) -> p c f", f=128),
                      s_v2[h].rearrange("(c p) f -> p c f", p=128)[:, 0:nk, :],
                      waits=[(dS, c_p1a_vdma)], tag=f"p2_vld_{j}_{h}")
            c_kv = dG.n

            for g in range(ng):
                spar = g % 2
                for ci in range(2):
                    c = 2 * g + ci
                    w = []
                    if ci == 0:
                        w = [(dG, c_kv), (dS, c_qt), (eS, c_s_free[spar]),
                             (eV, c_wo_ev)]
                    ten(lambda e, c=c, ci=ci, spar=spar, h=h, slot=slot:
                        e.matmul(ps_s(spar)[:, ci * 512:(ci + 1) * 512],
                                 ktsl(slot)[:, c * 128:(c + 1) * 128],
                                 p2_qt[:, h * 512:(h + 1) * 512],
                                 start=True, stop=True),
                        waits=w, inc=(ci == 1), tag=f"p2_smm_{j}_{h}_{g}")
                c_smm = eT.n
                c_e = act(lambda e, spar=spar: e.activation(
                    out=ptsl(spar), in_=ps_s(spar), func=AF.Exp,
                    scale=INVSQ, bias=t_bias[:, 0:1]),
                    waits=[(eT, c_smm), (eT, c_pt_free[spar]), (dG, c_mask)],
                    tag=f"p2_exp_{j}_{h}_{g}")
                c_s_free[spar] = c_e
                c_exp_lastpair[spar] = c_e
                u = g - (ng - 4)
                wz = [(eS, c_e)]
                if u >= 0:
                    c_m = vec(lambda e, spar=spar, u=u: e.tensor_mul(
                        ptsl(spar), ptsl(spar),
                        t_mask[:, u * 1024:(u + 1) * 1024]),
                        waits=[(eS, c_e)], tag=f"p2_mask_{j}_{h}_{g}")
                    wz = [(eV, c_m)]
                if g == 0:
                    wz = wz + [(eV, c_lz_free), (eV, c_const)]
                ten(lambda e, spar=spar, g=g: e.matmul(
                    ps_l, t_ones[:], ptsl(spar)[:, 0:512],
                    start=(g == 0), stop=False),
                    waits=wz, tag=f"p2_lmm_{j}_{h}_{g}")
                ten(lambda e, spar=spar, g=g, ng=ng: e.matmul(
                    ps_l, t_ones[:], ptsl(spar)[:, 512:1024],
                    start=False, stop=(g == ng - 1)), tag=f"p2_lmm2_{j}_{h}_{g}")
                for ci in range(2):
                    c = 2 * g + ci
                    ten(lambda e, c=c, ci=ci, spar=spar, slot=slot, nk=nk:
                        e.matmul(ps_zz[:],
                                 v2sl(slot)[:, c * 128:(c + 1) * 128],
                                 ptsl(spar)[:, ci * 512:(ci + 1) * 512],
                                 start=(c == 0), stop=(c == nk - 1)),
                        inc=(ci == 1), tag=f"p2_zmm_{j}_{h}_{g}")
                c_pt_free[spar] = eT.n
            c_kv_free[slot] = eT.n
            c_il = vec(lambda e: e.reciprocal(p2_il[:], ps_l),
                       waits=[(eT, c_kv_free[slot])], tag=f"p2_rcp_{j}_{h}")
            c_lz_free = vec(lambda e, h=h: e.scalar_tensor_tensor(
                p2_z[:, h * 512:(h + 1) * 512], ps_zz[:], 1.0, p2_il[:],
                ALU.mult, ALU.mult),
                waits=[(eV, c_il)], tag=f"p2_zev_{j}_{h}")

        c_aev = {}
        for qs in range(4):
            for hc in range(HC):
                for dq in range(4):
                    w = []
                    if hc == 0 and dq == 0:
                        w = [(eV, c_lz_free), (dG, c_wo),
                             (eS, c_s_free[0]), (eS, c_s_free[1])]
                        if qs > 0:
                            w.append((eV, c_aev[qs - 1]))
                        else:
                            w.append((eV, c_wo_ev))
                    ten(lambda e, qs=qs, hc=hc, dq=dq: e.matmul(
                        psA[:, dq * 512:(dq + 1) * 512],
                        p2_z[:, hc * 512 + qs * 128:hc * 512 + qs * 128 + 128],
                        p2_wo[:, hc * D + dq * 512:hc * D + (dq + 1) * 512],
                        start=(hc == 0), stop=(hc == HC - 1)),
                        waits=w, inc=(hc == HC - 1 and dq == 3),
                        tag=f"p2_womm_{j}_{qs}")
            cmm = eT.n
            rslot = qs % 2
            w = [(eT, cmm), (dS, c_xq[qs])]
            pk = (j, qs - 2) if qs >= 2 else ((j - 1, qs + 2) if j > 0 else None)
            if pk is not None and pk in c_r1_dma:
                w.append((dS, c_r1_dma[pk]))
            c_aev[qs] = vec(lambda e, qs=qs, rslot=rslot: e.tensor_add(
                p2_r1[:, rslot * D:(rslot + 1) * D], psA[:],
                p2_xq[:, qs * D:(qs + 1) * D]),
                waits=w, tag=f"p2_aev_{j}_{qs}")
            c_xq_free[qs] = (c_aev[qs], 0)
            c_r1_dma[(j, qs)] = dma(
                "sync", s_r1[j, qs],
                p2_r1[:, rslot * D:(rslot + 1) * D],
                waits=[(eV, c_aev[qs])], tag=f"p2_r1sp_{j}_{qs}")
        c_wo_ev = c_aev[3]

    c_p2_T = eT.n
    c_p2_r1_last = c_r1_dma[(NBLK - 1, 3)]

    n_groups = (IC + DOWN_G - 1) // DOWN_G
    c_gsil = {}
    c_hdn_v = {}
    c_dn_freeT = {}
    c_d_free = {0: 0, 1: 0}
    c_ost_dma = {}
    c_p3prev = c_wo_ev
    c_out_last = 0

    for j in range(NBLK):
        c_r1q = dma("sync", p3_r1[:].rearrange("p (q f) -> p q f", f=D),
                    s_r1[j].rearrange("q p f -> p q f"),
                    waits=[(dS, c_r1_dma[(j, 3)]), (eV, c_p3prev),
                           (eT, c_p2_T)],
                    tag=f"p3_r1ld_{j}")
        c_lnT_ready = 0
        c_lnfree3 = {0: c_p2_T, 1: c_p2_T}
        for tt in range(4):
            par = tt % 2
            r1_ap = p3_r1[:, tt * D:(tt + 1) * D]
            ln2_tm = p3_ost[:].bitcast(BF16)[:, 0:D]
            wx = [(dS, c_r1q)]
            wl = [(eT, c_lnfree3[par])]
            if j == 0 and tt == 0:
                wx += [(eS, c_exp_lastpair[0]), (eS, c_exp_lastpair[1]),
                       (dS, c_p2_r1_last), (eV, c_wo_ev)]
            elif tt == 0:
                wx += [(dS, c_ost_dma[(j - 1, 2)]), (dS, c_ost_dma[(j - 1, 3)])]
            a_sq, c_norm = emit_ln(
                r1_ap, ln2_tm, par, waits_x=wx, waits_ln=wx + wl,
                sq_discard=p3_hdn[:, 0:D], tag=f"p3_ln_{j}_{tt}")
            c_lnT_ready, c_tr = emit_transpose(
                ln2_tm, p3_lnT, tt, c_norm,
                extra_first=[(eV, c_p3prev)] if tt == 0 else (),
                tag=f"p3_{j}_{tt}")
            c_lnfree3[par] = c_tr

        for ic in range(IC):
            slot = ic % 2
            w = [(eT, c_p2_T)]
            kprev = (j, ic - 2) if ic >= 2 else ((j - 1, IC - 2 + ic) if j > 0 else None)
            if kprev is not None and kprev in c_gsil:
                w += [(eS, c_gsil[kprev]), (eV, c_hdn_v[kprev])]
            c_g = dma("gpsimd", p3_gu[:, slot * D:(slot + 1) * D], d_gate[ic],
                      waits=w, tag=f"p3_gld_{j}_{ic}")
            c_u = dma("gpsimd", p3_gu[:, (2 + slot) * D:(3 + slot) * D], d_up[ic],
                      tag=f"p3_uld_{j}_{ic}")
            c_gu = dG.n
            kp1 = (j, ic - 1) if ic >= 1 else ((j - 1, IC - 1) if j > 0 else None)
            for hc in range(HC):
                w = []
                if hc == 0:
                    w = [(eV, c_lnT_ready), (dG, c_g)]
                    if kp1 is not None and kp1 in c_gsil:
                        w.append((eS, c_gsil[kp1]))
                    if j == 0 and ic == 0:
                        w.append((eV, c_lz_free))
                ten(lambda e, hc=hc, slot=slot: e.matmul(
                    psB[:, 0:512],
                    p3_gu[:, slot * D + hc * 128:slot * D + (hc + 1) * 128],
                    p3_lnT[:, hc * 512:(hc + 1) * 512],
                    start=(hc == 0), stop=(hc == HC - 1)),
                    waits=w, inc=(hc == HC - 1), tag=f"p3_gmm_{j}_{ic}")
            c_gmm = eT.n
            for hc in range(HC):
                w = []
                if hc == 0:
                    w = [(dG, c_gu)]
                    if kp1 is not None and kp1 in c_hdn_v:
                        w.append((eV, c_hdn_v[kp1]))
                    if j == 0 and ic == 0:
                        w.append((eV, c_lz_free))
                ten(lambda e, hc=hc, slot=slot: e.matmul(
                    psB[:, 512:1024],
                    p3_gu[:, (2 + slot) * D + hc * 128:(2 + slot) * D + (hc + 1) * 128],
                    p3_lnT[:, hc * 512:(hc + 1) * 512],
                    start=(hc == 0), stop=(hc == HC - 1)),
                    waits=w, inc=(hc == HC - 1), tag=f"p3_umm_{j}_{ic}")
            c_umm = eT.n
            gslot = ic % 2
            w = [(eT, c_gmm), (dS, c_ds)]
            kprev2 = (j, ic - 2) if ic >= 2 else ((j - 1, IC - 2 + ic) if j > 0 else None)
            if kprev2 is not None and kprev2 in c_hdn_v:
                w.append((eV, c_hdn_v[kprev2]))
            if j == 0 and ic < 2:
                w.append((dS, c_p2_r1_last))
            c_gsil[(j, ic)] = act(lambda e, gslot=gslot, ic=ic: e.activation(
                out=p3_gub[:, gslot * 512:(gslot + 1) * 512],
                in_=psB[:, 0:512], func=AF.Silu, scale=t_gs[:, ic:ic + 1]),
                waits=w, tag=f"p3_silu_{j}_{ic}")
            c_hdn_v[(j, ic)] = vec(lambda e, gslot=gslot, ic=ic:
                                   e.scalar_tensor_tensor(
                p3_hdn[:, ic * 512:(ic + 1) * 512], psB[:, 512:1024],
                t_us[:, ic:ic + 1],
                p3_gub[:, gslot * 512:(gslot + 1) * 512],
                ALU.mult, ALU.mult),
                waits=[(eT, c_umm), (eS, c_gsil[(j, ic)])],
                tag=f"p3_hdn_{j}_{ic}")
        c_hdn_all = c_hdn_v[(j, IC - 1)]

        for g in range(n_groups):
            ics = list(range(g * DOWN_G, min((g + 1) * DOWN_G, IC)))
            slot = g % 2
            w = [(eT, c_p2_T)]
            kprev = (j, g - 2) if g >= 2 else ((j - 1, n_groups - 2 + g) if j > 0 else None)
            if kprev is not None and kprev in c_dn_freeT:
                w.append((eT, c_dn_freeT[kprev]))
            c_dn = dma(
                "gpsimd",
                p3_dn[:, slot * DOWN_G * D: slot * DOWN_G * D + len(ics) * D]
                .rearrange("p (i f) -> p i f", f=D),
                d_down[ics[0]:ics[0] + len(ics)].rearrange("i p f -> p i f"),
                waits=w, tag=f"p3_dnld_{j}_{g}")
            for qs in range(4):
                for hh in range(2):
                    hpar = (g * 8 + qs * 2 + hh) % 2
                    pd = psA[:, hpar * 1024:(hpar + 1) * 1024]
                    for ii, ic in enumerate(ics):
                        for dq in range(2):
                            w = []
                            if ii == 0 and dq == 0:
                                w = [(dG, c_dn), (eV, c_hdn_all),
                                     (eV, c_d_free[hpar])]
                                if j == 0 and g == 0 and qs == 0:
                                    w.append((eV, c_wo_ev))
                            ten(lambda e, ii=ii, ic=ic, qs=qs, dq=dq, hh=hh,
                                pd=pd, slot=slot, nics=len(ics): e.matmul(
                                pd[:, dq * 512:(dq + 1) * 512],
                                p3_hdn[:, ic * 512 + qs * 128:ic * 512 + qs * 128 + 128],
                                p3_dn[:, slot * DOWN_G * D + ii * D + hh * 1024 + dq * 512:
                                      slot * DOWN_G * D + ii * D + hh * 1024 + (dq + 1) * 512],
                                start=(ii == 0), stop=(ii == nics - 1)),
                                waits=w, inc=(ii == len(ics) - 1 and dq == 1),
                                tag=f"p3_dmm_{j}_{g}_{qs}_{hh}")
                    cdm = eT.n
                    dst = p3_acc[:, qs * D + hh * 1024: qs * D + (hh + 1) * 1024]
                    if g == 0:
                        c_d_free[hpar] = vec(
                            lambda e, dst=dst, pd=pd: e.tensor_copy(dst, pd),
                            waits=[(eT, cdm)], tag=f"p3_acc0_{j}_{g}_{qs}_{hh}")
                    else:
                        c_d_free[hpar] = vec(
                            lambda e, dst=dst, pd=pd: e.tensor_add(dst, dst, pd),
                            waits=[(eT, cdm)], tag=f"p3_acc_{j}_{g}_{qs}_{hh}")
                    c_acc_last = c_d_free[hpar]
            c_dn_freeT[(j, g)] = eT.n

        for qs in range(4):
            oslot = qs % 2
            ofs = oslot * D
            w = [(eV, c_acc_last), (dS, c_ds)]
            pk = (j, qs - 2) if qs >= 2 else ((j - 1, qs + 2) if j > 0 else None)
            if pk is not None and pk in c_ost_dma:
                w.append((dS, c_ost_dma[pk]))
            if j == 0 and qs < 2:
                w.append((eV, c_lnT_ready))
            cm = vec(lambda e, qs=qs, ofs=ofs: e.tensor_mul(
                p3_ost[:, ofs:ofs + D], p3_acc[:, qs * D:(qs + 1) * D], t_ds[:]),
                waits=w, tag=f"p3_mul_{j}_{qs}")
            ca = vec(lambda e, qs=qs, ofs=ofs: e.tensor_add(
                p3_ost[:, ofs:ofs + D], p3_ost[:, ofs:ofs + D],
                p3_r1[:, qs * D:(qs + 1) * D]),
                waits=[(eV, cm)], tag=f"p3_add_{j}_{qs}")
            t0 = j * 512 + qs * 128
            c_ost_dma[(j, qs)] = dma("sync", d_out[t0:t0 + 128, :],
                                     p3_ost[:, ofs:ofs + D],
                                     waits=[(eV, ca)], tag=f"p3_out_{j}_{qs}")
            c_out_last = c_ost_dma[(j, qs)]
        c_p3prev = eV.n

    if DEBUG:
        dbgs = [
            ("dbg_kt", s_kt, [NH, 128, T], BF16),
            ("dbg_v2", s_v2, [NH, T, 128], BF16),
            ("dbg_qt", s_qt, [NBLK, NH, 128, 512], BF16),
            ("dbg_r1", s_r1, [NBLK, 4, 128, D], BF16),
        ]
        for nm, src, shp, dt in dbgs:
            dd = nc.dram_tensor(nm, shp, dt, kind="ExternalOutput").ap()
            c_out_last = dma("sync", dd, src, waits=[(dS, c_out_last)],
                             tag=f"dbg_{nm}")
        for nm, src in (("dbg_acc", p3_acc), ("dbg_hdn", p3_hdn),
                        ("dbg_r1q", p3_r1), ("dbg_stat", t_stat),
                        ("dbg_mask", t_mask), ("dbg_lnT", p3_lnT)):
            ap = src[:]
            dd = nc.dram_tensor(nm, list(ap.shape), ap.dtype,
                                kind="ExternalOutput").ap()
            c_out_last = dma("sync", dd, ap, waits=[(dS, c_out_last)],
                             tag=f"dbg_{nm}")

    P.op("sync", lambda e: e.nop(), waits=[(dS, c_out_last)], tag="final")

    P.check()

    from contextlib import ExitStack
    ctx = ExitStack()
    for s in sems:
        s.h = ctx.enter_context(nc.semaphore(s.name))

    def run_engine(eng_obj, name):
        self_sem = {"vector": eV, "scalar": eS}.get(name)
        cnt = 0
        for fn, waits, incs, tag in P.ops[name]:
            if self_sem is not None and cnt > 0:
                eng_obj.wait_ge(self_sem.h, cnt)
            for s, v in waits:
                eng_obj.wait_ge(s.h, v)
            inst = fn(eng_obj)
            for s, amt in incs:
                inst.then_inc(s.h, amt)
                if s is self_sem:
                    cnt += amt

    with nc.Block() as block:
        @block.sync
        def _(e):
            run_engine(e, "sync")

        @block.scalar
        def _(e):
            run_engine(e, "scalar")

        @block.vector
        def _(e):
            run_engine(e, "vector")

        @block.tensor
        def _(e):
            run_engine(e, "tensor")

        @block.gpsimd
        def _(e):
            run_engine(e, "gpsimd")

    ctx.close()
    return nc


def _host_masks(role):
    k = np.arange(128)[:, None]
    q = np.arange(512)[None, :]
    m = np.zeros((128, 8 * 512), np.float32)
    for u in range(8):
        m[:, u * 512:(u + 1) * 512] = (q + 512 * role - 128 * u - k >= 0)
    return m


_NC_CACHE = [None]


def kernel(L0_attn_x, L0_attn_Wq, L0_attn_Wk, L0_attn_Wv, L0_attn_Wo,
           L0_ffn_gate_w, L0_ffn_up_w, L0_ffn_down_w, L0_ffn_gate_s,
           L0_ffn_up_s, L0_ffn_down_s):
    x = np.ascontiguousarray(np.asarray(L0_attn_x, dtype=np.float32))
    Wq = np.asarray(L0_attn_Wq, np.float32)
    Wk = np.asarray(L0_attn_Wk, np.float32)
    Wv = np.asarray(L0_attn_Wv, np.float32)
    Wo = np.asarray(L0_attn_Wo, np.float32)
    gw = np.asarray(L0_ffn_gate_w, np.float32)
    uw = np.asarray(L0_ffn_up_w, np.float32)
    dw = np.asarray(L0_ffn_down_w, np.float32)
    gs = np.asarray(L0_ffn_gate_s, np.float32)
    us = np.asarray(L0_ffn_up_s, np.float32)
    ds = np.asarray(L0_ffn_down_s, np.float32)

    def qk_img(W):
        A = W.reshape(HC, 128, HC, 128)
        return np.ascontiguousarray(A.transpose(3, 0, 2, 1)).reshape(128, -1)

    def vo_img(W):
        A = W.reshape(D, HC, 128)
        return np.ascontiguousarray(A.transpose(2, 1, 0)).reshape(128, -1)

    def gu_img(W):
        A = W.reshape(IC, 128, HC, 128)
        return np.ascontiguousarray(A.transpose(0, 3, 2, 1)).reshape(IC, 128, -1)

    def dn_img(W):
        A = W.reshape(D, IC, 128)
        return np.ascontiguousarray(A.transpose(1, 2, 0))

    wk_i = qk_img(Wk)
    wq_i = qk_img(Wq)
    wv_i = vo_img(Wv)
    wo_i = vo_img(Wo)
    g_i = gu_img(gw)
    u_i = gu_img(uw)
    d_i = dn_img(dw)
    gs_i = np.ascontiguousarray(gs.reshape(IC, 128).T)
    us_i = np.ascontiguousarray(us.reshape(IC, 128).T)
    ds_i = np.ascontiguousarray(np.broadcast_to(ds[None, :], (128, D)))
    ident = np.eye(128, dtype=np.float32)
    masks = [_host_masks(0), _host_masks(1)]

    in_maps = []
    for c in range(8):
        b, r = c // 2, c % 2
        rows = np.concatenate(
            [np.arange(512 * (2 * jj + r), 512 * (2 * jj + r) + 512)
             for jj in range(NBLK)])
        in_maps.append({
            "xkv": x[b],
            "xq": np.ascontiguousarray(x[b][rows]),
            "wk": wk_i, "wq": wq_i, "wv": wv_i, "wo": wo_i,
            "gate": g_i, "up": u_i, "down": d_i,
            "gs": gs_i, "us": us_i, "ds": ds_i,
            "maskin": masks[r], "identin": ident,
        })

    if _NC_CACHE[0] is None:
        _NC_CACHE[0] = build_program()
    nc = _NC_CACHE[0]
    res = run_bass_kernel_spmd(nc, in_maps, core_ids=list(range(8)))
    if DEBUG:
        _NC_CACHE.append(res)

    out = np.empty((B, T, D), np.float32)
    for c in range(8):
        b, r = c // 2, c % 2
        oc = res.results[c]["out"]
        for jj in range(NBLK):
            out[b, 512 * (2 * jj + r):512 * (2 * jj + r) + 512] = \
                oc[512 * jj:512 * (jj + 1)]
    return out


# version 10
# speedup vs baseline: 1.0144x; 1.0144x over previous
import numpy as np

import concourse.bass as bass
import concourse.mybir as mybir
from concourse.bass_utils import run_bass_kernel_spmd

F32 = mybir.dt.float32
BF16 = mybir.dt.bfloat16
AF = mybir.ActivationFunctionType
ALU = mybir.AluOpType
AXX = mybir.AxisListType.X

B = 4
T = 4096
D = 2048
NH = 16
HC = 16
INTER = 5632
IC = 44
OWN = 2048
NBLK = 4
NSLICE = 8
SHIFT = 24.0
EPS = 1e-5
INVSQ = float(1.0 / np.sqrt(128.0))
DOWN_G = 6

ENGS = ("sync", "scalar", "vector", "tensor", "gpsimd")

DEBUG = False


def nkc(j):
    return 8 * (j + 1)


class Sem:
    def __init__(self, name):
        self.name = name
        self.n = 0
        self.h = None


class Prog:
    def __init__(self):
        self.ops = {e: [] for e in ENGS}

    def op(self, eng, fn, waits=(), incs=(), tag=""):
        for s, amt in incs:
            s.n += amt
        self.ops[eng].append((fn, tuple(waits), tuple(incs), tag))

    def check(self):
        vals = {}
        ptr = {e: 0 for e in ENGS}
        progress = True
        while progress:
            progress = False
            for e in ENGS:
                while ptr[e] < len(self.ops[e]):
                    fn, waits, incs, tag = self.ops[e][ptr[e]]
                    if all(vals.get(s.name, 0) >= v for s, v in waits):
                        for s, amt in incs:
                            vals[s.name] = vals.get(s.name, 0) + amt
                        ptr[e] += 1
                        progress = True
                    else:
                        break
        stuck = {e: ptr[e] for e in ENGS if ptr[e] < len(self.ops[e])}
        if stuck:
            lines = []
            for e, i in stuck.items():
                fn, waits, incs, tag = self.ops[e][i]
                wd = [(s.name, v, vals.get(s.name, 0)) for s, v in waits]
                lines.append(f"{e}[{i}/{len(self.ops[e])}] tag={tag} waits={wd}")
            raise RuntimeError("schedule deadlock:\n" + "\n".join(lines))


def build_program():
    nc = bass.Bass("TRN2", target_bir_lowering=False, debug=False,
                   detect_race_conditions=False)

    d_xkv = nc.dram_tensor("xkv", [T, D], F32, kind="ExternalInput").ap()
    d_xq = nc.dram_tensor("xq", [OWN, D], F32, kind="ExternalInput").ap()
    d_wk = nc.dram_tensor("wk", [128, HC * HC * 128], F32, kind="ExternalInput").ap()
    d_wq = nc.dram_tensor("wq", [128, HC * HC * 128], F32, kind="ExternalInput").ap()
    d_wv = nc.dram_tensor("wv", [128, HC * D], F32, kind="ExternalInput").ap()
    d_wo = nc.dram_tensor("wo", [128, HC * D], F32, kind="ExternalInput").ap()
    d_gate = nc.dram_tensor("gate", [IC, 128, HC * 128], F32, kind="ExternalInput").ap()
    d_up = nc.dram_tensor("up", [IC, 128, HC * 128], F32, kind="ExternalInput").ap()
    d_down = nc.dram_tensor("down", [IC, 128, D], F32, kind="ExternalInput").ap()
    d_gs = nc.dram_tensor("gs", [128, IC], F32, kind="ExternalInput").ap()
    d_us = nc.dram_tensor("us", [128, IC], F32, kind="ExternalInput").ap()
    d_ds = nc.dram_tensor("ds", [128, D], F32, kind="ExternalInput").ap()
    d_mask = nc.dram_tensor("maskin", [128, 8 * 512], F32, kind="ExternalInput").ap()
    d_ident = nc.dram_tensor("identin", [128, 128], F32, kind="ExternalInput").ap()
    d_out = nc.dram_tensor("out", [OWN, D], F32, kind="ExternalOutput").ap()

    s_kt = nc.dram_tensor("s_kt", [NH, 128, T], BF16, kind="Internal").ap()
    s_v2 = nc.dram_tensor("s_v2", [NH, T, 128], BF16, kind="Internal").ap()
    s_qt = nc.dram_tensor("s_qt", [NBLK, NH, 128, 512], BF16, kind="Internal").ap()
    s_r1 = nc.dram_tensor("s_r1", [NBLK, 4, 128, D], BF16, kind="Internal").ap()

    base = (nc.sbuf_base + 511) & ~511
    TOP = nc.sbuf_top

    def sb(name, cols_bytes_off, shape, dtype):
        per = int(np.prod(shape[1:])) * mybir.dt.size(dtype)
        assert shape[0] == 128
        assert base + cols_bytes_off + per <= TOP, (name, cols_bytes_off, per)
        return nc.alloc_sbuf_tensor_at(name, list(shape), dtype,
                                       offset=base + cols_bytes_off)

    t_ident = sb("ident", 0x0000, [128, 128], BF16)
    t_mask = sb("mask8", 0x0200, [128, 8 * 512], BF16)
    t_gs = sb("gsb", 0x2200, [128, IC], F32)
    t_us = sb("usb", 0x2300, [128, IC], F32)
    t_ds = sb("dsb", 0x2400, [128, D], F32)
    t_bias = sb("biasc", 0x4400, [128, 4], F32)
    t_ones = sb("ones", 0x4420, [128, 128], BF16)
    t_stat = sb("stats", 0x4520, [128, 24], F32)
    CB = 0x4800

    p1_wk = sb("p1_wk", CB + 0x00000, [128, HC * HC * 128], BF16)
    p1_wv = sb("p1_wv", CB + 0x10000, [128, HC * D], BF16)
    p1_lnT = sb("p1_lnT", CB + 0x20000, [128, HC * 512], BF16)
    p1_x = sb("p1_x", CB + 0x24000, [128, 2 * D], F32)
    p1_ln = sb("p1_ln", CB + 0x28000, [128, 2 * D], BF16)
    p1_kst = sb("p1_kst", CB + 0x2A000, [128, HC * 512], BF16)
    p1_vst = sb("p1_vst", CB + 0x2E000, [128, D], BF16)
    p1b_wq = sb("p1b_wq", CB + 0x00000, [128, HC * HC * 128], BF16)
    p1b_lnT = sb("p1b_lnT", CB + 0x20000, [128, HC * 512], BF16)
    p1b_x = sb("p1b_x", CB + 0x24000, [128, 2 * D], F32)
    p1b_ln = sb("p1b_ln", CB + 0x28000, [128, 2 * D], BF16)
    p1b_qst = sb("p1b_qst", CB + 0x2A000, [128, HC * 512], BF16)
    p2_qt = sb("p2_qt", CB + 0x00000, [128, NH * 512], BF16)
    p2_kt = sb("p2_kt", CB + 0x04000, [128, 2 * T], BF16)
    p2_v2 = sb("p2_v2", CB + 0x08000, [128, 2 * T], BF16)
    p2_pt = sb("p2_pt", CB + 0x0C000, [128, 2 * 1024], BF16)
    p2_wo = sb("p2_wo", CB + 0x10000, [128, HC * D], BF16)
    p2_z = sb("p2_z", CB + 0x20000, [128, NH * 512], BF16)
    p2_xq = sb("p2_xq", CB + 0x24000, [128, 4 * D], F32)
    p2_il = sb("p2_il", CB + 0x2C000, [128, 512], F32)
    p2_r1 = sb("p2_r1", CB + 0x2C800, [128, 2 * D], BF16)
    p3_acc = sb("p3_acc", CB + 0x00000, [128, 4 * D], F32)
    p3_lnT = sb("p3_lnT", CB + 0x00000, [128, HC * 512], BF16)
    p3_hdn = sb("p3_hdn", CB + 0x08000, [128, IC * 512], BF16)
    p3_gu = sb("p3_gu", CB + 0x13000, [128, 4 * HC * 128], BF16)
    p3_dn = sb("p3_dn", CB + 0x17000, [128, 2 * DOWN_G * D], BF16)
    p3_r1 = sb("p3_r1", CB + 0x23000, [128, 4 * D], BF16)
    p3_ost = sb("p3_ost", CB + 0x27000, [128, 2 * D], F32)
    p3_gub = sb("p3_gub", CB + 0x2B000, [128, 2 * 512], BF16)

    psA = nc.alloc_psum_tensor("psA", [128, 2048], F32)
    psB = nc.alloc_psum_tensor("psB", [128, 1024], F32)
    psC = nc.alloc_psum_tensor("psC", [128, 512], F32)
    psT = nc.alloc_psum_tensor("psT", [128, 1024], BF16)

    sems = [Sem(n) for n in ("dS", "dG", "eT", "eV", "eS")]
    dS, dG, eT, eV, eS = sems
    P = Prog()

    def dma(q, out, in_, waits=(), tag=""):
        s = dS if q == "sync" else dG
        P.op(q, lambda e, out=out, in_=in_: e.dma_start(out=out, in_=in_),
             waits=waits, incs=[(s, 16)], tag=tag)
        return s.n

    def ten(fn, waits=(), inc=False, tag=""):
        P.op("tensor", fn, waits=waits, incs=[(eT, 1)] if inc else [], tag=tag)
        return eT.n

    def vec(fn, waits=(), tag=""):
        P.op("vector", fn, waits=waits, incs=[(eV, 1)], tag=tag)
        return eV.n

    def act(fn, waits=(), tag=""):
        P.op("scalar", fn, waits=waits, incs=[(eS, 1)], tag=tag)
        return eS.n

    dbg_taps = []

    def dbg_snap(name, ap):
        if not DEBUG:
            return
        dd = nc.dram_tensor(name, list(ap.shape), ap.dtype,
                            kind="ExternalOutput").ap()
        dma("sync", dd, ap,
            waits=[(eV, eV.n), (eS, eS.n), (eT, eT.n), (dG, dG.n)],
            tag=f"snap_{name}")
        dbg_taps.append(name)

    c_ident = dma("gpsimd", t_ident[:], d_ident, tag="ld_ident")
    c_mask = dma("gpsimd", t_mask[:], d_mask, tag="ld_mask")
    dma("sync", t_gs[:], d_gs, tag="ld_gs")
    dma("sync", t_us[:], d_us, tag="ld_us")
    c_ds = dma("sync", t_ds[:], d_ds, tag="ld_ds")
    P.op("gpsimd", lambda e: e.memset(t_bias[:], -SHIFT), tag="set_bias")
    P.op("gpsimd", lambda e: e.memset(t_ones[:], 1.0), incs=[(eV, 1)],
         tag="set_ones")
    c_const = eV.n

    def emit_ln(x_ap, ln_ap, par, waits_x, waits_ln, sq_discard, tag=""):
        bcol = 12 * par

        def col(i):
            return t_stat[:, bcol + i:bcol + i + 1]

        vec(lambda e, x_ap=x_ap: e.reduce_sum(col(0), x_ap, axis=AXX),
            waits=list(waits_x), tag=tag + "_red")
        a_sq = act(lambda e, x_ap=x_ap, sq_discard=sq_discard: e.activation(
            out=sq_discard, in_=x_ap, func=AF.Square),
            waits=list(waits_x) + list(waits_ln), tag=tag + "_sq")
        vec(lambda e, sq_discard=sq_discard: e.reduce_sum(
            col(1), sq_discard, axis=AXX),
            waits=[(eS, a_sq)], tag=tag + "_sqred")
        vec(lambda e: e.tensor_scalar(col(2), col(0), 1.0 / D, None, ALU.mult),
            tag=tag + "_mean")
        vec(lambda e: e.tensor_scalar(col(3), col(1), 1.0 / D, None, ALU.mult),
            tag=tag + "_ex2")
        vec(lambda e: e.tensor_mul(col(4), col(2), col(2)), tag=tag + "_m2")
        vec(lambda e: e.scalar_tensor_tensor(col(5), col(4), -1.0, col(3),
                                             ALU.mult, ALU.add), tag=tag + "_var")
        vec(lambda e: e.tensor_scalar(col(6), col(5), EPS, None, ALU.add),
            tag=tag + "_veps")
        c_rv = vec(lambda e: e.reciprocal(col(7), col(6)), tag=tag + "_rv")
        a_rstd = act(lambda e: e.activation(out=col(8), in_=col(7), func=AF.Sqrt),
                     waits=[(eV, c_rv)], tag=tag + "_rstd")
        c_norm = vec(lambda e, x_ap=x_ap, ln_ap=ln_ap: e.tensor_scalar(
            ln_ap, x_ap, col(2), col(8), ALU.subtract, ALU.mult),
            waits=[(eS, a_rstd)] + list(waits_ln), tag=tag + "_norm")
        return a_sq, c_norm

    def emit_transpose(ln_ap, lnT_h, tt, c_norm, extra_first=(), tag=""):
        last_ev = eV.n
        last_tr = eT.n
        for grp in range(4):
            for q in range(4):
                hc = grp * 4 + q
                w = []
                if grp == 0 and q == 0:
                    w = [(eV, c_norm), (dG, c_ident)] + list(extra_first)
                if q == 0 and grp > 0:
                    w.append((eV, last_ev))
                last_tr = ten(lambda e, hc=hc, q=q, ln_ap=ln_ap: e.transpose(
                    psT[:, q * 128:(q + 1) * 128],
                    ln_ap[:, hc * 128:(hc + 1) * 128], t_ident[:]),
                    waits=w, inc=(q == 3), tag=f"{tag}_tr{grp}")
            last_ev = vec(lambda e, grp=grp, tt=tt, lnT_h=lnT_h: e.tensor_copy(
                lnT_h[:].rearrange("p (hc f) -> p hc f", f=512)
                [:, grp * 4:(grp + 1) * 4, tt * 128:tt * 128 + 128],
                psT[:, 0:512].rearrange("p (q f) -> p q f", f=128)),
                waits=[(eT, last_tr)], tag=f"{tag}_ev{grp}")
        return last_ev, last_tr

    c_wk = dma("gpsimd", p1_wk[:], d_wk, tag="ld_wk")
    c_wv = dma("gpsimd", p1_wv[:], d_wv, tag="ld_wv")

    xsl = lambda b, slot: b[:, slot * D:(slot + 1) * D]
    c_xfree = {}
    c_lnfree = {0: 0, 1: 0}
    c_kev = {0: 0, 1: 0}
    c_kst_dma = 0
    c_vst_dma = 0
    c_vevac_prev = 0
    for s in range(NSLICE):
        c_lnT_ready = 0
        for tt in range(4):
            slot = tt % 2
            wx = []
            if slot in c_xfree:
                wx = [(eS, c_xfree[slot][0]), (eV, c_xfree[slot][1])]
            t0 = s * 512 + tt * 128
            c_x = dma("sync", xsl(p1_x, slot), d_xkv[t0:t0 + 128, :], waits=wx,
                      tag=f"p1a_x_{s}_{tt}")
            a_sq, c_norm = emit_ln(
                xsl(p1_x, slot), xsl(p1_ln, slot), slot,
                waits_x=[(dS, c_x)], waits_ln=[(eT, c_lnfree[slot])],
                sq_discard=xsl(p1_ln, slot), tag=f"p1a_ln_{s}_{tt}")
            c_xfree[slot] = (a_sq, c_norm)
            c_lnT_ready, c_tr = emit_transpose(
                xsl(p1_ln, slot), p1_lnT, tt, c_norm, tag=f"p1a_{s}_{tt}")
            c_lnfree[slot] = c_tr
        for dc in range(NH):
            par = dc % 2
            pk = psB[:, par * 512:(par + 1) * 512]
            for hc in range(HC):
                w = []
                if hc == 0:
                    w = [(eV, c_lnT_ready), (dG, c_wk), (eV, c_kev[par])]
                    if dc < 2:
                        w.append((dS, c_kst_dma))
                ten(lambda e, dc=dc, hc=hc, pk=pk: e.matmul(
                    pk, p1_wk[:, (dc * HC + hc) * 128:(dc * HC + hc + 1) * 128],
                    p1_lnT[:, hc * 512:(hc + 1) * 512],
                    start=(hc == 0), stop=(hc == HC - 1)),
                    waits=w, inc=(hc == HC - 1), tag=f"p1a_kmm_{s}_{dc}")
            cmm = eT.n
            c_kev[par] = vec(lambda e, dc=dc, pk=pk: e.tensor_copy(
                p1_kst[:, dc * 512:(dc + 1) * 512], pk),
                waits=[(eT, cmm)], tag=f"p1a_kev_{s}_{dc}")
        c_kst_dma = dma(
            "sync",
            s_kt.rearrange("h p k -> p h k")[:, :, s * 512:(s + 1) * 512],
            p1_kst[:].rearrange("p (h f) -> p h f", f=512),
            waits=[(eV, c_kev[0]), (eV, c_kev[1])], tag=f"p1a_kspill_{s}")
        if s == NSLICE - 1:
            c_afterK_last = eT.n
        for tt in range(4):
            for hc in range(HC):
                for dq in range(4):
                    w = []
                    if hc == 0 and dq == 0:
                        w = [(eV, c_lnT_ready), (dG, c_wv), (eS, c_vevac_prev)]
                    ten(lambda e, hc=hc, tt=tt, dq=dq: e.matmul(
                        psA[:, dq * 512:(dq + 1) * 512],
                        p1_lnT[:, hc * 512 + tt * 128:hc * 512 + tt * 128 + 128],
                        p1_wv[:, hc * D + dq * 512:hc * D + (dq + 1) * 512],
                        start=(hc == 0), stop=(hc == HC - 1)),
                        waits=w, inc=(hc == HC - 1 and dq == 3),
                        tag=f"p1a_vmm_{s}_{tt}")
            cmm = eT.n
            c_vevac_prev = act(lambda e: e.activation(
                out=p1_vst[:], in_=psA[:], func=AF.Copy),
                waits=[(eT, cmm), (dS, c_vst_dma)], tag=f"p1a_vev_{s}_{tt}")
            t0 = s * 512 + tt * 128
            c_vst_dma = dma(
                "sync",
                s_v2.rearrange("h t f -> t h f")[t0:t0 + 128, :, :],
                p1_vst[:].rearrange("p (h f) -> p h f", f=128),
                waits=[(eS, c_vevac_prev)], tag=f"p1a_vspill_{s}_{tt}")
        if s == 0:
            dbg_snap("dbg_p1x", p1_x[:])
            dbg_snap("dbg_p1ln", p1_ln[:])
            dbg_snap("dbg_p1lnT", p1_lnT[:])
            dbg_snap("dbg_p1wk", p1_wk[:])
            dbg_snap("dbg_p1wv", p1_wv[:])
            dbg_snap("dbg_p1kst", p1_kst[:])
            dbg_snap("dbg_p1vst", p1_vst[:])
            dbg_snap("dbg_stat0", t_stat[:])

    c_p1a_kdma = c_kst_dma
    c_p1a_vdma = c_vst_dma
    c_p1a_T = eT.n
    c_p1a_S = eS.n
    c_p1a_V = eV.n

    c_wq = dma("gpsimd", p1b_wq[:], d_wq, waits=[(eT, c_afterK_last)],
               tag="ld_wq")
    c_wo = dma("gpsimd", p2_wo[:], d_wo, waits=[(eT, c_p1a_T)], tag="ld_wo")

    c_xfree = {}
    c_lnfree = {0: c_p1a_T, 1: c_p1a_T}
    c_qev = {0: c_p1a_V, 1: c_p1a_V}
    c_qst_dma = c_p1a_kdma
    for j in range(NBLK):
        c_lnT_ready = 0
        for tt in range(4):
            slot = tt % 2
            wx = [(eS, c_p1a_S), (eV, c_p1a_V), (eT, c_p1a_T)] if j == 0 and tt < 2 \
                else [(eS, c_xfree[slot][0]), (eV, c_xfree[slot][1])]
            t0 = j * 512 + tt * 128
            c_x = dma("sync", xsl(p1b_x, slot), d_xq[t0:t0 + 128, :], waits=wx,
                      tag=f"p1b_x_{j}_{tt}")
            a_sq, c_norm = emit_ln(
                xsl(p1b_x, slot), xsl(p1b_ln, slot), slot,
                waits_x=[(dS, c_x)], waits_ln=[(eT, c_lnfree[slot])],
                sq_discard=xsl(p1b_ln, slot), tag=f"p1b_ln_{j}_{tt}")
            c_xfree[slot] = (a_sq, c_norm)
            c_lnT_ready, c_tr = emit_transpose(
                xsl(p1b_ln, slot), p1b_lnT, tt, c_norm, tag=f"p1b_{j}_{tt}")
            c_lnfree[slot] = c_tr
        for dc in range(NH):
            par = dc % 2
            pq = psB[:, par * 512:(par + 1) * 512]
            for hc in range(HC):
                w = []
                if hc == 0:
                    w = [(eV, c_lnT_ready), (dG, c_wq), (eV, c_qev[par])]
                    if dc < 2:
                        w.append((dS, c_qst_dma))
                ten(lambda e, dc=dc, hc=hc, pq=pq: e.matmul(
                    pq, p1b_wq[:, (dc * HC + hc) * 128:(dc * HC + hc + 1) * 128],
                    p1b_lnT[:, hc * 512:(hc + 1) * 512],
                    start=(hc == 0), stop=(hc == HC - 1)),
                    waits=w, inc=(hc == HC - 1), tag=f"p1b_qmm_{j}_{dc}")
            cmm = eT.n
            c_qev[par] = vec(lambda e, dc=dc, pq=pq: e.tensor_copy(
                p1b_qst[:, dc * 512:(dc + 1) * 512], pq),
                waits=[(eT, cmm)], tag=f"p1b_qev_{j}_{dc}")
        c_qst_dma = dma(
            "sync",
            s_qt[j].rearrange("h p f -> p h f"),
            p1b_qst[:].rearrange("p (h f) -> p h f", f=512),
            waits=[(eV, c_qev[0]), (eV, c_qev[1])], tag=f"p1b_qspill_{j}")

    c_p1b_qdma = c_qst_dma
    c_p1b_T = eT.n

    ktsl = lambda slot: p2_kt[:, slot * T:(slot + 1) * T]
    v2sl = lambda slot: p2_v2[:, slot * T:(slot + 1) * T]
    ptsl = lambda slot: p2_pt[:, slot * 1024:(slot + 1) * 1024]
    ps_s = lambda par: psA[:, par * 1024:(par + 1) * 1024]
    ps_l = psB[:, 0:512]
    ps_zz = psC

    c_s_free = {0: 0, 1: 0}
    c_pt_free = {0: c_p1b_T, 1: c_p1b_T}
    c_kv_free = {0: c_p1b_T, 1: c_p1b_T}
    c_lz_free = c_const
    c_wo_ev = c_const
    c_r1_dma = {}
    c_xq_free = {qs: (0, 0) for qs in range(4)}
    c_exp_lastpair = [0, 0]

    for j in range(NBLK):
        nk = nkc(j)
        ng = nk // 2
        kmax = nk * 128
        c_qt = dma("sync", p2_qt[:].rearrange("p (h f) -> p h f", f=512),
                   s_qt[j].rearrange("h p f -> p h f"),
                   waits=[(dS, c_p1b_qdma), (eT, c_p1b_T)], tag=f"p2_qt_{j}")
        c_xq = []
        for qs in range(4):
            t0 = j * 512 + qs * 128
            w = [(eT, c_p1b_T)] if j == 0 else [(eV, c_xq_free[qs][0])]
            c_xq.append(dma("sync", p2_xq[:, qs * D:(qs + 1) * D],
                            d_xq[t0:t0 + 128, :], waits=w,
                            tag=f"p2_xq_{j}_{qs}"))

        for h in range(NH):
            slot = h % 2
            c_k = dma("gpsimd", ktsl(slot)[:, 0:kmax], s_kt[h][:, 0:kmax],
                      waits=[(dS, c_p1a_kdma), (eT, c_kv_free[slot])],
                      tag=f"p2_kld_{j}_{h}")
            c_v = dma("gpsimd",
                      v2sl(slot)[:, 0:kmax].rearrange("p (c f) -> p c f", f=128),
                      s_v2[h].rearrange("(c p) f -> p c f", p=128)[:, 0:nk, :],
                      waits=[(dS, c_p1a_vdma), (eT, c_kv_free[slot])],
                      tag=f"p2_vld_{j}_{h}")
            c_kv = dG.n

            for g in range(ng):
                spar = g % 2
                for ci in range(2):
                    c = 2 * g + ci
                    w = []
                    if ci == 0:
                        w = [(dG, c_kv), (dS, c_qt), (eS, c_s_free[spar]),
                             (eV, c_wo_ev)]
                    ten(lambda e, c=c, ci=ci, spar=spar, h=h, slot=slot:
                        e.matmul(ps_s(spar)[:, ci * 512:(ci + 1) * 512],
                                 ktsl(slot)[:, c * 128:(c + 1) * 128],
                                 p2_qt[:, h * 512:(h + 1) * 512],
                                 start=True, stop=True),
                        waits=w, inc=(ci == 1), tag=f"p2_smm_{j}_{h}_{g}")
                c_smm = eT.n
                c_e = act(lambda e, spar=spar: e.activation(
                    out=ptsl(spar), in_=ps_s(spar), func=AF.Exp,
                    scale=INVSQ, bias=t_bias[:, 0:1]),
                    waits=[(eT, c_smm), (eT, c_pt_free[spar]), (dG, c_mask)],
                    tag=f"p2_exp_{j}_{h}_{g}")
                c_s_free[spar] = c_e
                c_exp_lastpair[spar] = c_e
                u = g - (ng - 4)
                wz = [(eS, c_e)]
                if u >= 0:
                    c_m = vec(lambda e, spar=spar, u=u: e.tensor_mul(
                        ptsl(spar), ptsl(spar),
                        t_mask[:, u * 1024:(u + 1) * 1024]),
                        waits=[(eS, c_e)], tag=f"p2_mask_{j}_{h}_{g}")
                    wz = [(eV, c_m)]
                if g == 0:
                    wz = wz + [(eV, c_lz_free), (eV, c_const)]
                ten(lambda e, spar=spar, g=g: e.matmul(
                    ps_l, t_ones[:], ptsl(spar)[:, 0:512],
                    start=(g == 0), stop=False),
                    waits=wz, tag=f"p2_lmm_{j}_{h}_{g}")
                ten(lambda e, spar=spar, g=g, ng=ng: e.matmul(
                    ps_l, t_ones[:], ptsl(spar)[:, 512:1024],
                    start=False, stop=(g == ng - 1)), tag=f"p2_lmm2_{j}_{h}_{g}")
                for ci in range(2):
                    c = 2 * g + ci
                    ten(lambda e, c=c, ci=ci, spar=spar, slot=slot, nk=nk:
                        e.matmul(ps_zz[:],
                                 v2sl(slot)[:, c * 128:(c + 1) * 128],
                                 ptsl(spar)[:, ci * 512:(ci + 1) * 512],
                                 start=(c == 0), stop=(c == nk - 1)),
                        inc=(ci == 1), tag=f"p2_zmm_{j}_{h}_{g}")
                c_pt_free[spar] = eT.n
            c_kv_free[slot] = eT.n
            c_il = vec(lambda e: e.reciprocal(p2_il[:], ps_l),
                       waits=[(eT, c_kv_free[slot])], tag=f"p2_rcp_{j}_{h}")
            c_lz_free = vec(lambda e, h=h: e.scalar_tensor_tensor(
                p2_z[:, h * 512:(h + 1) * 512], ps_zz[:], 1.0, p2_il[:],
                ALU.mult, ALU.mult),
                waits=[(eV, c_il)], tag=f"p2_zev_{j}_{h}")

        c_aev = {}
        for qs in range(4):
            for hc in range(HC):
                for dq in range(4):
                    w = []
                    if hc == 0 and dq == 0:
                        w = [(eV, c_lz_free), (dG, c_wo),
                             (eS, c_s_free[0]), (eS, c_s_free[1])]
                        if qs > 0:
                            w.append((eV, c_aev[qs - 1]))
                        else:
                            w.append((eV, c_wo_ev))
                    ten(lambda e, qs=qs, hc=hc, dq=dq: e.matmul(
                        psA[:, dq * 512:(dq + 1) * 512],
                        p2_z[:, hc * 512 + qs * 128:hc * 512 + qs * 128 + 128],
                        p2_wo[:, hc * D + dq * 512:hc * D + (dq + 1) * 512],
                        start=(hc == 0), stop=(hc == HC - 1)),
                        waits=w, inc=(hc == HC - 1 and dq == 3),
                        tag=f"p2_womm_{j}_{qs}")
            cmm = eT.n
            rslot = qs % 2
            w = [(eT, cmm), (dS, c_xq[qs])]
            pk = (j, qs - 2) if qs >= 2 else ((j - 1, qs + 2) if j > 0 else None)
            if pk is not None and pk in c_r1_dma:
                w.append((dS, c_r1_dma[pk]))
            c_aev[qs] = vec(lambda e, qs=qs, rslot=rslot: e.tensor_add(
                p2_r1[:, rslot * D:(rslot + 1) * D], psA[:],
                p2_xq[:, qs * D:(qs + 1) * D]),
                waits=w, tag=f"p2_aev_{j}_{qs}")
            c_xq_free[qs] = (c_aev[qs], 0)
            c_r1_dma[(j, qs)] = dma(
                "sync", s_r1[j, qs],
                p2_r1[:, rslot * D:(rslot + 1) * D],
                waits=[(eV, c_aev[qs])], tag=f"p2_r1sp_{j}_{qs}")
        c_wo_ev = c_aev[3]

    c_p2_T = eT.n
    c_p2_r1_last = c_r1_dma[(NBLK - 1, 3)]

    n_groups = (IC + DOWN_G - 1) // DOWN_G
    c_gsil = {}
    c_hdn_v = {}
    c_dn_freeT = {}
    c_d_free = {0: 0, 1: 0}
    c_ost_dma = {}
    c_p3prev = c_wo_ev
    c_out_last = 0

    for j in range(NBLK):
        c_r1q = dma("sync", p3_r1[:].rearrange("p (q f) -> p q f", f=D),
                    s_r1[j].rearrange("q p f -> p q f"),
                    waits=[(dS, c_r1_dma[(j, 3)]), (eV, c_p3prev),
                           (eT, c_p2_T)],
                    tag=f"p3_r1ld_{j}")
        c_lnT_ready = 0
        c_lnfree3 = {0: c_p2_T, 1: c_p2_T}
        for tt in range(4):
            par = tt % 2
            r1_ap = p3_r1[:, tt * D:(tt + 1) * D]
            ln2_tm = p3_ost[:].bitcast(BF16)[:, 0:D]
            wx = [(dS, c_r1q)]
            wl = [(eT, c_lnfree3[par])]
            if j == 0 and tt == 0:
                wx += [(eS, c_exp_lastpair[0]), (eS, c_exp_lastpair[1]),
                       (dS, c_p2_r1_last), (eV, c_wo_ev)]
            elif tt == 0:
                wx += [(dS, c_ost_dma[(j - 1, 2)]), (dS, c_ost_dma[(j - 1, 3)])]
            a_sq, c_norm = emit_ln(
                r1_ap, ln2_tm, par, waits_x=wx, waits_ln=wx + wl,
                sq_discard=p3_hdn[:, 0:D], tag=f"p3_ln_{j}_{tt}")
            c_lnT_ready, c_tr = emit_transpose(
                ln2_tm, p3_lnT, tt, c_norm,
                extra_first=[(eV, c_p3prev)] if tt == 0 else (),
                tag=f"p3_{j}_{tt}")
            c_lnfree3[par] = c_tr

        for ic in range(IC):
            slot = ic % 2
            w = [(eT, c_p2_T)]
            kprev = (j, ic - 2) if ic >= 2 else ((j - 1, IC - 2 + ic) if j > 0 else None)
            if kprev is not None and kprev in c_gsil:
                w += [(eS, c_gsil[kprev]), (eV, c_hdn_v[kprev])]
            c_g = dma("gpsimd", p3_gu[:, slot * D:(slot + 1) * D], d_gate[ic],
                      waits=w, tag=f"p3_gld_{j}_{ic}")
            c_u = dma("gpsimd", p3_gu[:, (2 + slot) * D:(3 + slot) * D], d_up[ic],
                      tag=f"p3_uld_{j}_{ic}")
            c_gu = dG.n
            kp1 = (j, ic - 1) if ic >= 1 else ((j - 1, IC - 1) if j > 0 else None)
            for hc in range(HC):
                w = []
                if hc == 0:
                    w = [(eV, c_lnT_ready), (dG, c_g)]
                    if kp1 is not None and kp1 in c_gsil:
                        w.append((eS, c_gsil[kp1]))
                    if j == 0 and ic == 0:
                        w.append((eV, c_lz_free))
                ten(lambda e, hc=hc, slot=slot: e.matmul(
                    psB[:, 0:512],
                    p3_gu[:, slot * D + hc * 128:slot * D + (hc + 1) * 128],
                    p3_lnT[:, hc * 512:(hc + 1) * 512],
                    start=(hc == 0), stop=(hc == HC - 1)),
                    waits=w, inc=(hc == HC - 1), tag=f"p3_gmm_{j}_{ic}")
            c_gmm = eT.n
            for hc in range(HC):
                w = []
                if hc == 0:
                    w = [(dG, c_gu)]
                    if kp1 is not None and kp1 in c_hdn_v:
                        w.append((eV, c_hdn_v[kp1]))
                    if j == 0 and ic == 0:
                        w.append((eV, c_lz_free))
                ten(lambda e, hc=hc, slot=slot: e.matmul(
                    psB[:, 512:1024],
                    p3_gu[:, (2 + slot) * D + hc * 128:(2 + slot) * D + (hc + 1) * 128],
                    p3_lnT[:, hc * 512:(hc + 1) * 512],
                    start=(hc == 0), stop=(hc == HC - 1)),
                    waits=w, inc=(hc == HC - 1), tag=f"p3_umm_{j}_{ic}")
            c_umm = eT.n
            gslot = ic % 2
            w = [(eT, c_gmm), (dS, c_ds)]
            kprev2 = (j, ic - 2) if ic >= 2 else ((j - 1, IC - 2 + ic) if j > 0 else None)
            if kprev2 is not None and kprev2 in c_hdn_v:
                w.append((eV, c_hdn_v[kprev2]))
            if j == 0 and ic < 2:
                w.append((dS, c_p2_r1_last))
            c_gsil[(j, ic)] = act(lambda e, gslot=gslot, ic=ic: e.activation(
                out=p3_gub[:, gslot * 512:(gslot + 1) * 512],
                in_=psB[:, 0:512], func=AF.Silu, scale=t_gs[:, ic:ic + 1]),
                waits=w, tag=f"p3_silu_{j}_{ic}")
            c_hdn_v[(j, ic)] = vec(lambda e, gslot=gslot, ic=ic:
                                   e.scalar_tensor_tensor(
                p3_hdn[:, ic * 512:(ic + 1) * 512], psB[:, 512:1024],
                t_us[:, ic:ic + 1],
                p3_gub[:, gslot * 512:(gslot + 1) * 512],
                ALU.mult, ALU.mult),
                waits=[(eT, c_umm), (eS, c_gsil[(j, ic)])],
                tag=f"p3_hdn_{j}_{ic}")
        c_hdn_all = c_hdn_v[(j, IC - 1)]

        for g in range(n_groups):
            ics = list(range(g * DOWN_G, min((g + 1) * DOWN_G, IC)))
            slot = g % 2
            w = [(eT, c_p2_T)]
            kprev = (j, g - 2) if g >= 2 else ((j - 1, n_groups - 2 + g) if j > 0 else None)
            if kprev is not None and kprev in c_dn_freeT:
                w.append((eT, c_dn_freeT[kprev]))
            c_dn = dma(
                "gpsimd",
                p3_dn[:, slot * DOWN_G * D: slot * DOWN_G * D + len(ics) * D]
                .rearrange("p (i f) -> p i f", f=D),
                d_down[ics[0]:ics[0] + len(ics)].rearrange("i p f -> p i f"),
                waits=w, tag=f"p3_dnld_{j}_{g}")
            for qs in range(4):
                for hh in range(2):
                    hpar = (g * 8 + qs * 2 + hh) % 2
                    pd = psA[:, hpar * 1024:(hpar + 1) * 1024]
                    for ii, ic in enumerate(ics):
                        for dq in range(2):
                            w = []
                            if ii == 0 and dq == 0:
                                w = [(dG, c_dn), (eV, c_hdn_all),
                                     (eV, c_d_free[hpar])]
                                if j == 0 and g == 0 and qs == 0:
                                    w.append((eV, c_wo_ev))
                            ten(lambda e, ii=ii, ic=ic, qs=qs, dq=dq, hh=hh,
                                pd=pd, slot=slot, nics=len(ics): e.matmul(
                                pd[:, dq * 512:(dq + 1) * 512],
                                p3_hdn[:, ic * 512 + qs * 128:ic * 512 + qs * 128 + 128],
                                p3_dn[:, slot * DOWN_G * D + ii * D + hh * 1024 + dq * 512:
                                      slot * DOWN_G * D + ii * D + hh * 1024 + (dq + 1) * 512],
                                start=(ii == 0), stop=(ii == nics - 1)),
                                waits=w, inc=(ii == len(ics) - 1 and dq == 1),
                                tag=f"p3_dmm_{j}_{g}_{qs}_{hh}")
                    cdm = eT.n
                    dst = p3_acc[:, qs * D + hh * 1024: qs * D + (hh + 1) * 1024]
                    if g == 0:
                        c_d_free[hpar] = vec(
                            lambda e, dst=dst, pd=pd: e.tensor_copy(dst, pd),
                            waits=[(eT, cdm)], tag=f"p3_acc0_{j}_{g}_{qs}_{hh}")
                    else:
                        c_d_free[hpar] = vec(
                            lambda e, dst=dst, pd=pd: e.tensor_add(dst, dst, pd),
                            waits=[(eT, cdm)], tag=f"p3_acc_{j}_{g}_{qs}_{hh}")
                    c_acc_last = c_d_free[hpar]
            c_dn_freeT[(j, g)] = eT.n

        for qs in range(4):
            oslot = qs % 2
            ofs = oslot * D
            w = [(eV, c_acc_last), (dS, c_ds)]
            pk = (j, qs - 2) if qs >= 2 else ((j - 1, qs + 2) if j > 0 else None)
            if pk is not None and pk in c_ost_dma:
                w.append((dS, c_ost_dma[pk]))
            if j == 0 and qs < 2:
                w.append((eV, c_lnT_ready))
            cm = vec(lambda e, qs=qs, ofs=ofs: e.tensor_mul(
                p3_ost[:, ofs:ofs + D], p3_acc[:, qs * D:(qs + 1) * D], t_ds[:]),
                waits=w, tag=f"p3_mul_{j}_{qs}")
            ca = vec(lambda e, qs=qs, ofs=ofs: e.tensor_add(
                p3_ost[:, ofs:ofs + D], p3_ost[:, ofs:ofs + D],
                p3_r1[:, qs * D:(qs + 1) * D]),
                waits=[(eV, cm)], tag=f"p3_add_{j}_{qs}")
            t0 = j * 512 + qs * 128
            c_ost_dma[(j, qs)] = dma("sync", d_out[t0:t0 + 128, :],
                                     p3_ost[:, ofs:ofs + D],
                                     waits=[(eV, ca)], tag=f"p3_out_{j}_{qs}")
            c_out_last = c_ost_dma[(j, qs)]
        c_p3prev = eV.n

    if DEBUG:
        dbgs = [
            ("dbg_kt", s_kt, [NH, 128, T], BF16),
            ("dbg_v2", s_v2, [NH, T, 128], BF16),
            ("dbg_qt", s_qt, [NBLK, NH, 128, 512], BF16),
            ("dbg_r1", s_r1, [NBLK, 4, 128, D], BF16),
        ]
        for nm, src, shp, dt in dbgs:
            dd = nc.dram_tensor(nm, shp, dt, kind="ExternalOutput").ap()
            c_out_last = dma("sync", dd, src, waits=[(dS, c_out_last)],
                             tag=f"dbg_{nm}")
        for nm, src in (("dbg_acc", p3_acc), ("dbg_hdn", p3_hdn),
                        ("dbg_r1q", p3_r1), ("dbg_stat", t_stat),
                        ("dbg_mask", t_mask), ("dbg_lnT", p3_lnT)):
            ap = src[:]
            dd = nc.dram_tensor(nm, list(ap.shape), ap.dtype,
                                kind="ExternalOutput").ap()
            c_out_last = dma("sync", dd, ap, waits=[(dS, c_out_last)],
                             tag=f"dbg_{nm}")

    P.op("sync", lambda e: e.nop(), waits=[(dS, c_out_last)], tag="final")

    P.check()

    from contextlib import ExitStack
    ctx = ExitStack()
    for s in sems:
        s.h = ctx.enter_context(nc.semaphore(s.name))

    def run_engine(eng_obj, name):
        self_sem = {"vector": eV, "scalar": eS}.get(name)
        cnt = 0
        for fn, waits, incs, tag in P.ops[name]:
            if self_sem is not None and cnt > 0:
                eng_obj.wait_ge(self_sem.h, cnt)
            for s, v in waits:
                eng_obj.wait_ge(s.h, v)
            inst = fn(eng_obj)
            for s, amt in incs:
                inst.then_inc(s.h, amt)
                if s is self_sem:
                    cnt += amt

    with nc.Block() as block:
        @block.sync
        def _(e):
            run_engine(e, "sync")

        @block.scalar
        def _(e):
            run_engine(e, "scalar")

        @block.vector
        def _(e):
            run_engine(e, "vector")

        @block.tensor
        def _(e):
            run_engine(e, "tensor")

        @block.gpsimd
        def _(e):
            run_engine(e, "gpsimd")

    ctx.close()
    return nc


def _host_masks(role):
    k = np.arange(128)[:, None]
    q = np.arange(512)[None, :]
    m = np.zeros((128, 8 * 512), np.float32)
    for u in range(8):
        m[:, u * 512:(u + 1) * 512] = (q + 512 * role - 128 * u - k >= 0)
    return m


_NC_CACHE = [None]


def kernel(L0_attn_x, L0_attn_Wq, L0_attn_Wk, L0_attn_Wv, L0_attn_Wo,
           L0_ffn_gate_w, L0_ffn_up_w, L0_ffn_down_w, L0_ffn_gate_s,
           L0_ffn_up_s, L0_ffn_down_s):
    x = np.ascontiguousarray(np.asarray(L0_attn_x, dtype=np.float32))
    Wq = np.asarray(L0_attn_Wq, np.float32)
    Wk = np.asarray(L0_attn_Wk, np.float32)
    Wv = np.asarray(L0_attn_Wv, np.float32)
    Wo = np.asarray(L0_attn_Wo, np.float32)
    gw = np.asarray(L0_ffn_gate_w, np.float32)
    uw = np.asarray(L0_ffn_up_w, np.float32)
    dw = np.asarray(L0_ffn_down_w, np.float32)
    gs = np.asarray(L0_ffn_gate_s, np.float32)
    us = np.asarray(L0_ffn_up_s, np.float32)
    ds = np.asarray(L0_ffn_down_s, np.float32)

    def qk_img(W):
        A = W.reshape(HC, 128, HC, 128)
        return np.ascontiguousarray(A.transpose(3, 0, 2, 1)).reshape(128, -1)

    def vo_img(W):
        A = W.reshape(D, HC, 128)
        return np.ascontiguousarray(A.transpose(2, 1, 0)).reshape(128, -1)

    def gu_img(W):
        A = W.reshape(IC, 128, HC, 128)
        return np.ascontiguousarray(A.transpose(0, 3, 2, 1)).reshape(IC, 128, -1)

    def dn_img(W):
        A = W.reshape(D, IC, 128)
        return np.ascontiguousarray(A.transpose(1, 2, 0))

    wk_i = qk_img(Wk)
    wq_i = qk_img(Wq)
    wv_i = vo_img(Wv)
    wo_i = vo_img(Wo)
    g_i = gu_img(gw)
    u_i = gu_img(uw)
    d_i = dn_img(dw)
    gs_i = np.ascontiguousarray(gs.reshape(IC, 128).T)
    us_i = np.ascontiguousarray(us.reshape(IC, 128).T)
    ds_i = np.ascontiguousarray(np.broadcast_to(ds[None, :], (128, D)))
    ident = np.eye(128, dtype=np.float32)
    masks = [_host_masks(0), _host_masks(1)]

    in_maps = []
    for c in range(8):
        b, r = c // 2, c % 2
        rows = np.concatenate(
            [np.arange(512 * (2 * jj + r), 512 * (2 * jj + r) + 512)
             for jj in range(NBLK)])
        in_maps.append({
            "xkv": x[b],
            "xq": np.ascontiguousarray(x[b][rows]),
            "wk": wk_i, "wq": wq_i, "wv": wv_i, "wo": wo_i,
            "gate": g_i, "up": u_i, "down": d_i,
            "gs": gs_i, "us": us_i, "ds": ds_i,
            "maskin": masks[r], "identin": ident,
        })

    if _NC_CACHE[0] is None:
        _NC_CACHE[0] = build_program()
    nc = _NC_CACHE[0]
    res = run_bass_kernel_spmd(nc, in_maps, core_ids=list(range(8)))
    if DEBUG:
        _NC_CACHE.append(res)

    out = np.empty((B, T, D), np.float32)
    for c in range(8):
        b, r = c // 2, c % 2
        oc = res.results[c]["out"]
        for jj in range(NBLK):
            out[b, 512 * (2 * jj + r):512 * (2 * jj + r) + 512] = \
                oc[512 * jj:512 * (jj + 1)]
    return out


# version 18
# speedup vs baseline: 1.2192x; 1.2019x over previous
import numpy as np

import concourse.bass as bass
import concourse.mybir as mybir
from concourse.bass_utils import run_bass_kernel_spmd

F32 = mybir.dt.float32
BF16 = mybir.dt.bfloat16
AF = mybir.ActivationFunctionType
ALU = mybir.AluOpType
AXX = mybir.AxisListType.X

B = 4
T = 4096
D = 2048
NH = 16
HC = 16
INTER = 5632
IC = 44
OWN = 2048
NBLK = 4
NSLICE = 8
SHIFT = 24.0
EPS = 1e-5
INVSQ = float(1.0 / np.sqrt(128.0))
DOWN_G = 6

ENGS = ("sync", "scalar", "vector", "tensor", "gpsimd")

DEBUG = False
TRUNC_P1 = False
TRUNC_P1A = False


def nkc(j):
    return 8 * (j + 1)


class Sem:
    def __init__(self, name):
        self.name = name
        self.n = 0
        self.h = None


class Prog:
    def __init__(self):
        self.ops = {e: [] for e in ENGS}

    def op(self, eng, fn, waits=(), incs=(), tag=""):
        for s, amt in incs:
            s.n += amt
        self.ops[eng].append((fn, tuple(waits), tuple(incs), tag))

    def check(self):
        vals = {}
        ptr = {e: 0 for e in ENGS}
        progress = True
        while progress:
            progress = False
            for e in ENGS:
                while ptr[e] < len(self.ops[e]):
                    fn, waits, incs, tag = self.ops[e][ptr[e]]
                    if all(vals.get(s.name, 0) >= v for s, v in waits):
                        for s, amt in incs:
                            vals[s.name] = vals.get(s.name, 0) + amt
                        ptr[e] += 1
                        progress = True
                    else:
                        break
        stuck = {e: ptr[e] for e in ENGS if ptr[e] < len(self.ops[e])}
        if stuck:
            lines = []
            for e, i in stuck.items():
                fn, waits, incs, tag = self.ops[e][i]
                wd = [(s.name, v, vals.get(s.name, 0)) for s, v in waits]
                lines.append(f"{e}[{i}/{len(self.ops[e])}] tag={tag} waits={wd}")
            raise RuntimeError("schedule deadlock:\n" + "\n".join(lines))


def build_program():
    nc = bass.Bass("TRN2", target_bir_lowering=False, debug=False,
                   detect_race_conditions=False)

    d_xkv = nc.dram_tensor("xkv", [T, D], F32, kind="ExternalInput").ap()
    d_xq = nc.dram_tensor("xq", [OWN, D], F32, kind="ExternalInput").ap()
    d_wk = nc.dram_tensor("wk", [128, HC * HC * 128], F32, kind="ExternalInput").ap()
    d_wq = nc.dram_tensor("wq", [128, HC * HC * 128], F32, kind="ExternalInput").ap()
    d_wv = nc.dram_tensor("wv", [128, HC * D], F32, kind="ExternalInput").ap()
    d_wo = nc.dram_tensor("wo", [128, HC * D], F32, kind="ExternalInput").ap()
    d_gate = nc.dram_tensor("gate", [IC, 128, HC * 128], F32, kind="ExternalInput").ap()
    d_up = nc.dram_tensor("up", [IC, 128, HC * 128], F32, kind="ExternalInput").ap()
    d_down = nc.dram_tensor("down", [IC, 128, D], F32, kind="ExternalInput").ap()
    d_gs = nc.dram_tensor("gs", [128, IC], F32, kind="ExternalInput").ap()
    d_us = nc.dram_tensor("us", [128, IC], F32, kind="ExternalInput").ap()
    d_ds = nc.dram_tensor("ds", [128, D], F32, kind="ExternalInput").ap()
    d_mask = nc.dram_tensor("maskin", [128, 8 * 512], F32, kind="ExternalInput").ap()
    d_ident = nc.dram_tensor("identin", [128, 128], F32, kind="ExternalInput").ap()
    d_out = nc.dram_tensor("out", [OWN, D], F32, kind="ExternalOutput").ap()

    s_kt = nc.dram_tensor("s_kt", [NH, 128, T], BF16, kind="Internal").ap()
    s_v2 = nc.dram_tensor("s_v2", [T, NH * 128], BF16, kind="Internal").ap()
    s_qt = nc.dram_tensor("s_qt", [NBLK, NH, 128, 512], BF16, kind="Internal").ap()
    s_r1 = nc.dram_tensor("s_r1", [NBLK, 4, 128, D], BF16, kind="Internal").ap()

    base = (nc.sbuf_base + 511) & ~511
    TOP = nc.sbuf_top

    def sb(name, cols_bytes_off, shape, dtype):
        per = int(np.prod(shape[1:])) * mybir.dt.size(dtype)
        assert shape[0] == 128
        assert base + cols_bytes_off + per <= TOP, (name, cols_bytes_off, per)
        return nc.alloc_sbuf_tensor_at(name, list(shape), dtype,
                                       offset=base + cols_bytes_off)

    t_ident = sb("ident", 0x0000, [128, 128], BF16)
    t_mask = sb("mask8", 0x0200, [128, 8 * 512], BF16)
    t_gs = sb("gsb", 0x2200, [128, IC], F32)
    t_us = sb("usb", 0x2300, [128, IC], F32)
    t_ds = sb("dsb", 0x2400, [128, D], F32)
    t_bias = sb("biasc", 0x4400, [128, 4], F32)
    t_ones = sb("ones", 0x4420, [128, 128], BF16)
    t_stat = sb("stats", 0x4520, [128, 40], F32)
    CB = 0x4800

    p1_wk = sb("p1_wk", CB + 0x00000, [128, HC * HC * 128], BF16)
    p1_wv = sb("p1_wv", CB + 0x10000, [128, HC * D], BF16)
    p1_lnT = sb("p1_lnT", CB + 0x20000, [128, HC * 512], BF16)
    p1_x = sb("p1_x", CB + 0x24000, [128, 2 * D], F32)
    p1_ln = sb("p1_ln", CB + 0x28000, [128, 4 * D], BF16)
    p1_kst = sb("p1_kst", CB + 0x2C000, [128, 8 * 512], BF16)
    p1_vst = sb("p1_vst", CB + 0x2E000, [128, D], BF16)
    p1b_wq = sb("p1b_wq", CB + 0x00000, [128, HC * HC * 128], BF16)
    p1b_lnT = sb("p1b_lnT", CB + 0x20000, [128, HC * 512], BF16)
    p1b_x = sb("p1b_x", CB + 0x24000, [128, 2 * D], F32)
    p1b_ln = sb("p1b_ln", CB + 0x28000, [128, 4 * D], BF16)
    p1b_qst = sb("p1b_qst", CB + 0x2C000, [128, 8 * 512], BF16)
    p2_qt = sb("p2_qt", CB + 0x00000, [128, NH * 512], BF16)
    p2_kt = sb("p2_kt", CB + 0x04000, [128, 2 * T], BF16)
    p2_v2 = sb("p2_v2", CB + 0x08000, [128, 2 * T], BF16)
    p2_pt = sb("p2_pt", CB + 0x0C000, [128, 2 * 1024], BF16)
    p2_wo = sb("p2_wo", CB + 0x10000, [128, HC * D], BF16)
    p2_z = sb("p2_z", CB + 0x20000, [128, NH * 512], BF16)
    p2_xq = sb("p2_xq", CB + 0x24000, [128, 4 * D], F32)
    p2_il = sb("p2_il", CB + 0x2C000, [128, 512], F32)
    p2_r1 = sb("p2_r1", CB + 0x2C800, [128, 2 * D], BF16)
    p3_acc = sb("p3_acc", CB + 0x00000, [128, 4 * D], F32)
    p3_lnT = sb("p3_lnT", CB + 0x00000, [128, HC * 512], BF16)
    p3_hdn = sb("p3_hdn", CB + 0x08000, [128, IC * 512], BF16)
    p3_gu = sb("p3_gu", CB + 0x13000, [128, 4 * HC * 128], BF16)
    p3_dn = sb("p3_dn", CB + 0x17000, [128, 2 * DOWN_G * D], BF16)
    p3_r1 = sb("p3_r1", CB + 0x23000, [128, 4 * D], BF16)
    p3_ost = sb("p3_ost", CB + 0x27000, [128, 2 * D], F32)
    p3_gub = sb("p3_gub", CB + 0x2B000, [128, 2 * 512], BF16)

    psA = nc.alloc_psum_tensor("psA", [128, 2048], F32)
    psB = nc.alloc_psum_tensor("psB", [128, 1024], F32)
    psC = nc.alloc_psum_tensor("psC", [128, 512], F32)
    psT = nc.alloc_psum_tensor("psT", [128, 1024], BF16)

    sems = [Sem(n) for n in ("dS", "dG", "eT", "eV", "eS")]
    dS, dG, eT, eV, eS = sems
    P = Prog()

    def dma(q, out, in_, waits=(), tag=""):
        s = dS if q == "sync" else dG
        P.op(q, lambda e, out=out, in_=in_: e.dma_start(out=out, in_=in_),
             waits=waits, incs=[(s, 16)], tag=tag)
        return s.n

    def ten(fn, waits=(), inc=False, tag=""):
        P.op("tensor", fn, waits=waits, incs=[(eT, 1)] if inc else [], tag=tag)
        return eT.n

    def vec(fn, waits=(), tag=""):
        P.op("vector", fn, waits=waits, incs=[(eV, 1)], tag=tag)
        return eV.n

    def act(fn, waits=(), tag=""):
        P.op("scalar", fn, waits=waits, incs=[(eS, 1)], tag=tag)
        return eS.n

    dbg_taps = []

    def dbg_snap(name, ap):
        if not DEBUG:
            return
        dd = nc.dram_tensor(name, list(ap.shape), ap.dtype,
                            kind="ExternalOutput").ap()
        dma("sync", dd, ap,
            waits=[(eV, eV.n), (eS, eS.n), (eT, eT.n), (dG, dG.n)],
            tag=f"snap_{name}")
        dbg_taps.append(name)

    c_ident = dma("gpsimd", t_ident[:], d_ident, tag="ld_ident")
    c_mask = dma("gpsimd", t_mask[:], d_mask, tag="ld_mask")
    dma("sync", t_gs[:], d_gs, tag="ld_gs")
    dma("sync", t_us[:], d_us, tag="ld_us")
    c_ds = dma("sync", t_ds[:], d_ds, tag="ld_ds")
    P.op("gpsimd", lambda e: e.memset(t_bias[:], -SHIFT), tag="set_bias")
    P.op("gpsimd", lambda e: e.memset(t_ones[:], 1.0), incs=[(eV, 1)],
         tag="set_ones")
    c_const = eV.n

    def emit_ln2(x_aps, ln_aps, waits_x, waits_ln, tag=""):
        sqs = []
        for u in range(2):
            vec(lambda e, u=u: e.reduce_sum(t_stat[:, u:u + 1], x_aps[u],
                                            axis=AXX),
                waits=list(waits_x[u]), tag=tag + f"_red{u}")
            a_sq = act(lambda e, u=u: e.activation(
                out=ln_aps[u], in_=x_aps[u], func=AF.Square),
                waits=list(waits_x[u]) + list(waits_ln[u]),
                tag=tag + f"_sq{u}")
            vec(lambda e, u=u: e.reduce_sum(t_stat[:, 2 + u:3 + u],
                                            ln_aps[u], axis=AXX),
                waits=[(eS, a_sq)], tag=tag + f"_sqred{u}")
            sqs.append(a_sq)
        vec(lambda e: e.tensor_scalar(t_stat[:, 4:6], t_stat[:, 0:2],
                                      1.0 / D, None, ALU.mult),
            tag=tag + "_mean")
        vec(lambda e: e.tensor_scalar(t_stat[:, 6:8], t_stat[:, 2:4],
                                      1.0 / D, None, ALU.mult),
            tag=tag + "_ex2")
        vec(lambda e: e.tensor_scalar(t_stat[:, 6:8], t_stat[:, 6:8],
                                      EPS, None, ALU.add),
            tag=tag + "_eps")
        vec(lambda e: e.tensor_mul(t_stat[:, 8:10], t_stat[:, 4:6],
                                   t_stat[:, 4:6]), tag=tag + "_m2")
        vec(lambda e: e.scalar_tensor_tensor(t_stat[:, 10:12], t_stat[:, 8:10],
                                             -1.0, t_stat[:, 6:8],
                                             ALU.mult, ALU.add),
            tag=tag + "_var")
        c_rv = vec(lambda e: e.reciprocal(t_stat[:, 12:14], t_stat[:, 10:12]),
                   tag=tag + "_rv")
        a_rstd = act(lambda e: e.activation(out=t_stat[:, 14:16],
                                            in_=t_stat[:, 12:14], func=AF.Sqrt),
                     waits=[(eV, c_rv)], tag=tag + "_rstd")
        norms = []
        for u in range(2):
            cn = vec(lambda e, u=u: e.tensor_scalar(
                ln_aps[u], x_aps[u], t_stat[:, 4 + u:5 + u],
                t_stat[:, 14 + u:15 + u], ALU.subtract, ALU.mult),
                waits=[(eS, a_rstd)] + list(waits_ln[u]),
                tag=tag + f"_norm{u}")
            norms.append(cn)
        return sqs, norms

    def emit_ln_old(x_ap, ln_ap, par, waits_x, waits_ln, sq_discard, tag=""):
        bcol = 16 + 12 * par

        def col(i):
            return t_stat[:, bcol + i:bcol + i + 1]

        vec(lambda e, x_ap=x_ap: e.reduce_sum(col(0), x_ap, axis=AXX),
            waits=list(waits_x), tag=tag + "_red")
        a_sq = act(lambda e, x_ap=x_ap, sq_discard=sq_discard: e.activation(
            out=sq_discard, in_=x_ap, func=AF.Square),
            waits=list(waits_x) + list(waits_ln), tag=tag + "_sq")
        vec(lambda e, sq_discard=sq_discard: e.reduce_sum(
            col(1), sq_discard, axis=AXX),
            waits=[(eS, a_sq)], tag=tag + "_sqred")
        vec(lambda e: e.tensor_scalar(col(2), col(0), 1.0 / D, None, ALU.mult),
            tag=tag + "_mean")
        vec(lambda e: e.tensor_scalar(col(3), col(1), 1.0 / D, None, ALU.mult),
            tag=tag + "_ex2")
        vec(lambda e: e.tensor_mul(col(4), col(2), col(2)), tag=tag + "_m2")
        vec(lambda e: e.scalar_tensor_tensor(col(5), col(4), -1.0, col(3),
                                             ALU.mult, ALU.add), tag=tag + "_var")
        vec(lambda e: e.tensor_scalar(col(6), col(5), EPS, None, ALU.add),
            tag=tag + "_veps")
        c_rv = vec(lambda e: e.reciprocal(col(7), col(6)), tag=tag + "_rv")
        a_rstd = act(lambda e: e.activation(out=col(8), in_=col(7), func=AF.Sqrt),
                     waits=[(eV, c_rv)], tag=tag + "_rstd")
        c_norm = vec(lambda e, x_ap=x_ap, ln_ap=ln_ap: e.tensor_scalar(
            ln_ap, x_ap, col(2), col(8), ALU.subtract, ALU.mult),
            waits=[(eS, a_rstd)] + list(waits_ln), tag=tag + "_norm")
        return a_sq, c_norm

    def emit_transpose(ln_ap, lnT_h, tt, c_norm, extra_first=(), tag=""):
        last_ev = eV.n
        last_tr = eT.n
        for grp in range(4):
            for q in range(4):
                hc = grp * 4 + q
                w = []
                if grp == 0 and q == 0:
                    w = [(eV, c_norm), (dG, c_ident), (eV, last_ev)] + \
                        list(extra_first)
                if q == 0 and grp > 0:
                    w.append((eV, last_ev))
                last_tr = ten(lambda e, hc=hc, q=q, ln_ap=ln_ap: e.transpose(
                    psT[:, q * 128:(q + 1) * 128],
                    ln_ap[:, hc * 128:(hc + 1) * 128], t_ident[:]),
                    waits=w, inc=(q == 3), tag=f"{tag}_tr{grp}")
            last_ev = vec(lambda e, grp=grp, tt=tt, lnT_h=lnT_h: e.tensor_copy(
                lnT_h[:].rearrange("p (hc f) -> p hc f", f=512)
                [:, grp * 4:(grp + 1) * 4, tt * 128:tt * 128 + 128],
                psT[:, 0:512].rearrange("p (q f) -> p q f", f=128)),
                waits=[(eT, last_tr)], tag=f"{tag}_ev{grp}")
        return last_ev, last_tr

    c_wk = dma("gpsimd", p1_wk[:], d_wk, tag="ld_wk")
    c_wv = dma("gpsimd", p1_wv[:], d_wv, tag="ld_wv")

    xsl = lambda b, slot: b[:, slot * D:(slot + 1) * D]
    c_xfree = {}
    c_lnfree = {tt: 0 for tt in range(4)}
    c_kev = {0: 0, 1: 0}
    c_kst_half = {0: 0, 1: 0}
    c_vst_dma = 0
    c_vevac_prev = 0
    c_norms = {}

    def p1a_ln_chain(s):
        for pair in range(2):
            wxs, wls = [], []
            for u in range(2):
                tt = pair * 2 + u
                slot = tt % 2
                wx = []
                if slot in c_xfree:
                    wx = [(eS, c_xfree[slot][0]), (eV, c_xfree[slot][1])]
                t0 = s * 512 + tt * 128
                c_x = dma("sync", xsl(p1_x, slot), d_xkv[t0:t0 + 128, :],
                          waits=wx, tag=f"p1a_x_{s}_{tt}")
                wxs.append([(dS, c_x)])
                wls.append([(eT, c_lnfree[tt])])
            sqs, norms = emit_ln2(
                [xsl(p1_x, u) for u in range(2)],
                [xsl(p1_ln, pair * 2 + u) for u in range(2)],
                wxs, wls, tag=f"p1a_ln_{s}_{pair}")
            for u in range(2):
                tt = pair * 2 + u
                c_xfree[tt % 2] = (sqs[u], norms[u])
                c_norms[(s, tt)] = norms[u]

    p1a_ln_chain(0)
    for s in range(NSLICE):
        c_lnT_ready = 0
        for tt in range(4):
            c_lnT_ready, c_tr = emit_transpose(
                xsl(p1_ln, tt), p1_lnT, tt, c_norms[(s, tt)],
                tag=f"p1a_{s}_{tt}")
            c_lnfree[tt] = c_tr
        if s + 1 < NSLICE:
            p1a_ln_chain(s + 1)
        for dc in range(NH):
            par = dc % 2
            half = dc // 8
            pk = psB[:, par * 512:(par + 1) * 512]
            for hc in range(HC):
                w = []
                if hc == 0:
                    w = [(eV, c_lnT_ready), (dG, c_wk), (eS, c_kev[par])]
                ten(lambda e, dc=dc, hc=hc, pk=pk: e.matmul(
                    pk, p1_wk[:, (dc * HC + hc) * 128:(dc * HC + hc + 1) * 128],
                    p1_lnT[:, hc * 512:(hc + 1) * 512],
                    start=(hc == 0), stop=(hc == HC - 1)),
                    waits=w, inc=(hc == HC - 1), tag=f"p1a_kmm_{s}_{dc}")
            cmm = eT.n
            c_kev[par] = act(lambda e, dc=dc, pk=pk: e.activation(
                out=p1_kst[:, (dc % 8) * 512:(dc % 8 + 1) * 512], in_=pk,
                func=AF.Copy),
                waits=[(eT, cmm), (dS, c_kst_half[1 - half])],
                tag=f"p1a_kev_{s}_{dc}")
            if dc % 8 == 7:
                c_kst_half[half] = dma(
                    "sync",
                    s_kt.rearrange("h p k -> p h k")
                    [:, half * 8:(half + 1) * 8, s * 512:(s + 1) * 512],
                    p1_kst[:].rearrange("p (h f) -> p h f", f=512),
                    waits=[(eS, c_kev[0]), (eS, c_kev[1])],
                    tag=f"p1a_kspill_{s}_{half}")
        c_kst_dma = c_kst_half[1]
        if s == NSLICE - 1:
            c_afterK_last = eT.n
        for tt in range(4):
            for hc in range(HC):
                for dq in range(4):
                    w = []
                    if hc == 0 and dq == 0:
                        w = [(eV, c_lnT_ready), (dG, c_wv), (eS, c_vevac_prev)]
                    ten(lambda e, hc=hc, tt=tt, dq=dq: e.matmul(
                        psA[:, dq * 512:(dq + 1) * 512],
                        p1_lnT[:, hc * 512 + tt * 128:hc * 512 + tt * 128 + 128],
                        p1_wv[:, hc * D + dq * 512:hc * D + (dq + 1) * 512],
                        start=(hc == 0), stop=(hc == HC - 1)),
                        waits=w, inc=(hc == HC - 1 and dq == 3),
                        tag=f"p1a_vmm_{s}_{tt}")
            cmm = eT.n
            c_vevac_prev = act(lambda e: e.activation(
                out=p1_vst[:], in_=psA[:], func=AF.Copy),
                waits=[(eT, cmm), (dS, c_vst_dma)], tag=f"p1a_vev_{s}_{tt}")
            t0 = s * 512 + tt * 128
            c_vst_dma = dma(
                "sync",
                s_v2[t0:t0 + 128, :],
                p1_vst[:],
                waits=[(eS, c_vevac_prev)], tag=f"p1a_vspill_{s}_{tt}")

    c_p1a_kdma = c_kst_dma
    c_p1a_vdma = c_vst_dma
    c_p1a_T = eT.n
    c_p1a_S = eS.n
    c_p1a_V = eV.n

    if TRUNC_P1A:
        P.op("sync", lambda e: e.nop(),
             waits=[(dS, c_p1a_kdma), (dS, c_p1a_vdma)], tag="final")
        P.check()
        from contextlib import ExitStack
        ctx = ExitStack()
        for s in sems:
            s.h = ctx.enter_context(nc.semaphore(s.name))
        _emit_all(nc, P, sems)
        ctx.close()
        return nc

    c_wq = dma("gpsimd", p1b_wq[:], d_wq, waits=[(eT, c_afterK_last)],
               tag="ld_wq")
    c_wo = dma("gpsimd", p2_wo[:], d_wo, waits=[(eT, c_p1a_T)], tag="ld_wo")

    c_xfree = {}
    c_lnfree = {tt: c_p1a_T for tt in range(4)}
    c_qev = {0: c_p1a_S, 1: c_p1a_S}
    c_qst_half = {0: c_p1a_kdma, 1: c_p1a_kdma}
    c_normsb = {}

    def p1b_ln_chain(j):
        for pair in range(2):
            wxs, wls = [], []
            for u in range(2):
                tt = pair * 2 + u
                slot = tt % 2
                wx = [(eS, c_p1a_S), (eV, c_p1a_V)] if (j == 0 and tt < 2) \
                    else [(eS, c_xfree[slot][0]), (eV, c_xfree[slot][1])]
                t0 = j * 512 + tt * 128
                c_x = dma("sync", xsl(p1b_x, slot), d_xq[t0:t0 + 128, :],
                          waits=wx, tag=f"p1b_x_{j}_{tt}")
                wxs.append([(dS, c_x)])
                wls.append([(eT, c_lnfree[tt])])
            sqs, norms = emit_ln2(
                [xsl(p1b_x, u) for u in range(2)],
                [xsl(p1b_ln, pair * 2 + u) for u in range(2)],
                wxs, wls, tag=f"p1b_ln_{j}_{pair}")
            for u in range(2):
                tt = pair * 2 + u
                c_xfree[tt % 2] = (sqs[u], norms[u])
                c_normsb[(j, tt)] = norms[u]

    p1b_ln_chain(0)
    for j in range(NBLK):
        c_lnT_ready = 0
        for tt in range(4):
            c_lnT_ready, c_tr = emit_transpose(
                xsl(p1b_ln, tt), p1b_lnT, tt, c_normsb[(j, tt)],
                extra_first=[(eT, c_p1a_T)] if (j == 0 and tt == 0) else (),
                tag=f"p1b_{j}_{tt}")
            c_lnfree[tt] = c_tr
        if j + 1 < NBLK:
            p1b_ln_chain(j + 1)
        for dc in range(NH):
            par = dc % 2
            half = dc // 8
            pq = psB[:, par * 512:(par + 1) * 512]
            for hc in range(HC):
                w = []
                if hc == 0:
                    w = [(eV, c_lnT_ready), (dG, c_wq), (eS, c_qev[par])]
                ten(lambda e, dc=dc, hc=hc, pq=pq: e.matmul(
                    pq, p1b_wq[:, (dc * HC + hc) * 128:(dc * HC + hc + 1) * 128],
                    p1b_lnT[:, hc * 512:(hc + 1) * 512],
                    start=(hc == 0), stop=(hc == HC - 1)),
                    waits=w, inc=(hc == HC - 1), tag=f"p1b_qmm_{j}_{dc}")
            cmm = eT.n
            c_qev[par] = act(lambda e, dc=dc, pq=pq: e.activation(
                out=p1b_qst[:, (dc % 8) * 512:(dc % 8 + 1) * 512], in_=pq,
                func=AF.Copy),
                waits=[(eT, cmm), (dS, c_qst_half[1 - half])],
                tag=f"p1b_qev_{j}_{dc}")
            if dc % 8 == 7:
                c_qst_half[half] = dma(
                    "sync",
                    s_qt[j].rearrange("h p f -> p h f")
                    [:, half * 8:(half + 1) * 8, :],
                    p1b_qst[:].rearrange("p (h f) -> p h f", f=512),
                    waits=[(eS, c_qev[0]), (eS, c_qev[1])],
                    tag=f"p1b_qspill_{j}_{half}")
        c_qst_dma = c_qst_half[1]

    c_p1b_qdma = c_qst_dma
    c_p1b_T = eT.n

    if TRUNC_P1:
        P.op("sync", lambda e: e.nop(), waits=[(dS, c_p1b_qdma)], tag="final")
        P.check()
        from contextlib import ExitStack
        ctx = ExitStack()
        for s in sems:
            s.h = ctx.enter_context(nc.semaphore(s.name))
        _emit_all(nc, P, sems)
        ctx.close()
        return nc

    ktsl = lambda slot: p2_kt[:, slot * T:(slot + 1) * T]
    v2sl = lambda slot: p2_v2[:, slot * T:(slot + 1) * T]
    ptsl = lambda slot: p2_pt[:, slot * 1024:(slot + 1) * 1024]
    ps_s = lambda par: psA[:, par * 1024:(par + 1) * 1024]
    ps_l = psB[:, 0:512]
    ps_zz = psC

    c_s_free = {0: 0, 1: 0}
    c_pt_free = {0: c_p1b_T, 1: c_p1b_T}
    c_kv_free = {0: c_p1b_T, 1: c_p1b_T}
    c_lz_free = c_const
    c_wo_ev = c_const
    c_r1_dma = {}
    c_xq_free = {qs: (0, 0) for qs in range(4)}
    c_exp_lastpair = [0, 0]

    for j in range(NBLK):
        nk = nkc(j)
        ng = nk // 2
        kmax = nk * 128
        c_qt = dma("sync", p2_qt[:].rearrange("p (h f) -> p h f", f=512),
                   s_qt[j].rearrange("h p f -> p h f"),
                   waits=[(dS, c_p1b_qdma), (eT, c_p1b_T)], tag=f"p2_qt_{j}")
        c_xq = []
        for qs in range(4):
            t0 = j * 512 + qs * 128
            w = [(eT, c_p1b_T)] if j == 0 else [(eV, c_xq_free[qs][0])]
            c_xq.append(dma("sync", p2_xq[:, qs * D:(qs + 1) * D],
                            d_xq[t0:t0 + 128, :], waits=w,
                            tag=f"p2_xq_{j}_{qs}"))

        for h in range(NH):
            slot = h % 2
            c_k = dma("gpsimd", ktsl(slot)[:, 0:kmax], s_kt[h][:, 0:kmax],
                      waits=[(dS, c_p1a_kdma), (eT, c_kv_free[slot])],
                      tag=f"p2_kld_{j}_{h}")
            c_v = dma("gpsimd",
                      v2sl(slot)[:, 0:kmax].rearrange("p (c f) -> p c f", f=128),
                      s_v2.rearrange("(c p) d -> p c d", p=128)
                      [:, 0:nk, h * 128:(h + 1) * 128],
                      waits=[(dS, c_p1a_vdma), (eT, c_kv_free[slot])],
                      tag=f"p2_vld_{j}_{h}")
            c_kv = dG.n

            def emit_smm(g):
                spar = g % 2
                for ci in range(2):
                    c = 2 * g + ci
                    w = []
                    if ci == 0:
                        w = [(dG, c_kv), (dS, c_qt), (eS, c_s_free[spar]),
                             (eV, c_wo_ev)]
                    ten(lambda e, c=c, ci=ci, spar=spar, h=h, slot=slot:
                        e.matmul(ps_s(spar)[:, ci * 512:(ci + 1) * 512],
                                 ktsl(slot)[:, c * 128:(c + 1) * 128],
                                 p2_qt[:, h * 512:(h + 1) * 512],
                                 start=True, stop=True),
                        waits=w, inc=(ci == 1), tag=f"p2_smm_{j}_{h}_{g}")
                c_smm = eT.n
                c_e = act(lambda e, spar=spar: e.activation(
                    out=ptsl(spar), in_=ps_s(spar), func=AF.Exp,
                    scale=INVSQ, bias=t_bias[:, 0:1]),
                    waits=[(eT, c_smm), (eT, c_pt_free[spar]), (dG, c_mask)],
                    tag=f"p2_exp_{j}_{h}_{g}")
                c_s_free[spar] = c_e
                c_exp_lastpair[spar] = c_e
                u = g - (ng - 4)
                wz = [(eS, c_e)]
                if u >= 0:
                    c_m = vec(lambda e, spar=spar, u=u: e.tensor_mul(
                        ptsl(spar), ptsl(spar),
                        t_mask[:, u * 1024:(u + 1) * 1024]),
                        waits=[(eS, c_e)], tag=f"p2_mask_{j}_{h}_{g}")
                    wz = [(eV, c_m)]
                return wz

            def emit_lz(g, wz):
                spar = g % 2
                if g == 0:
                    wz = wz + [(eV, c_lz_free), (eV, c_const)]
                ten(lambda e, spar=spar, g=g: e.matmul(
                    ps_l, t_ones[:], ptsl(spar)[:, 0:512],
                    start=(g == 0), stop=False),
                    waits=wz, tag=f"p2_lmm_{j}_{h}_{g}")
                ten(lambda e, spar=spar, g=g, ng=ng: e.matmul(
                    ps_l, t_ones[:], ptsl(spar)[:, 512:1024],
                    start=False, stop=(g == ng - 1)), tag=f"p2_lmm2_{j}_{h}_{g}")
                for ci in range(2):
                    c = 2 * g + ci
                    ten(lambda e, c=c, ci=ci, spar=spar, slot=slot, nk=nk:
                        e.matmul(ps_zz[:],
                                 v2sl(slot)[:, c * 128:(c + 1) * 128],
                                 ptsl(spar)[:, ci * 512:(ci + 1) * 512],
                                 start=(c == 0), stop=(c == nk - 1)),
                        inc=(ci == 1), tag=f"p2_zmm_{j}_{h}_{g}")
                c_pt_free[spar] = eT.n

            wz_prev = emit_smm(0)
            for g in range(1, ng):
                wz = emit_smm(g)
                emit_lz(g - 1, wz_prev)
                wz_prev = wz
            emit_lz(ng - 1, wz_prev)
            c_kv_free[slot] = eT.n
            c_il = vec(lambda e: e.reciprocal(p2_il[:], ps_l),
                       waits=[(eT, c_kv_free[slot])], tag=f"p2_rcp_{j}_{h}")
            c_lz_free = vec(lambda e, h=h: e.scalar_tensor_tensor(
                p2_z[:, h * 512:(h + 1) * 512], ps_zz[:], 1.0, p2_il[:],
                ALU.mult, ALU.mult),
                waits=[(eV, c_il)], tag=f"p2_zev_{j}_{h}")

        c_aev = {}
        for qs in range(4):
            for hc in range(HC):
                for dq in range(4):
                    w = []
                    if hc == 0 and dq == 0:
                        w = [(eV, c_lz_free), (dG, c_wo),
                             (eS, c_s_free[0]), (eS, c_s_free[1])]
                        if qs > 0:
                            w.append((eV, c_aev[qs - 1]))
                        else:
                            w.append((eV, c_wo_ev))
                    ten(lambda e, qs=qs, hc=hc, dq=dq: e.matmul(
                        psA[:, dq * 512:(dq + 1) * 512],
                        p2_z[:, hc * 512 + qs * 128:hc * 512 + qs * 128 + 128],
                        p2_wo[:, hc * D + dq * 512:hc * D + (dq + 1) * 512],
                        start=(hc == 0), stop=(hc == HC - 1)),
                        waits=w, inc=(hc == HC - 1 and dq == 3),
                        tag=f"p2_womm_{j}_{qs}")
            cmm = eT.n
            rslot = qs % 2
            w = [(eT, cmm), (dS, c_xq[qs])]
            pk = (j, qs - 2) if qs >= 2 else ((j - 1, qs + 2) if j > 0 else None)
            if pk is not None and pk in c_r1_dma:
                w.append((dS, c_r1_dma[pk]))
            c_aev[qs] = vec(lambda e, qs=qs, rslot=rslot: e.tensor_add(
                p2_r1[:, rslot * D:(rslot + 1) * D], psA[:],
                p2_xq[:, qs * D:(qs + 1) * D]),
                waits=w, tag=f"p2_aev_{j}_{qs}")
            c_xq_free[qs] = (c_aev[qs], 0)
            c_r1_dma[(j, qs)] = dma(
                "sync", s_r1[j, qs],
                p2_r1[:, rslot * D:(rslot + 1) * D],
                waits=[(eV, c_aev[qs])], tag=f"p2_r1sp_{j}_{qs}")
        c_wo_ev = c_aev[3]

    c_p2_T = eT.n
    c_p2_r1_last = c_r1_dma[(NBLK - 1, 3)]

    n_groups = (IC + DOWN_G - 1) // DOWN_G
    c_gsil = {}
    c_hdn_v = {}
    c_dn_freeT = {}
    c_d_free = {0: 0, 1: 0}
    c_ost_dma = {}
    c_p3prev = c_wo_ev
    c_out_last = 0

    for j in range(NBLK):
        c_r1q = dma("sync", p3_r1[:].rearrange("p (q f) -> p q f", f=D),
                    s_r1[j].rearrange("q p f -> p q f"),
                    waits=[(dS, c_r1_dma[(j, 3)]), (eV, c_p3prev),
                           (eT, c_p2_T)],
                    tag=f"p3_r1ld_{j}")
        c_lnT_ready = 0
        c_lnfree3 = {0: c_p2_T, 1: c_p2_T}
        ln2v = p3_ost[:].bitcast(BF16)
        for pair in range(2):
            wxs, wls = [], []
            for u in range(2):
                tt = pair * 2 + u
                wx = [(dS, c_r1q)]
                if j == 0 and tt == 0:
                    wx += [(eS, c_exp_lastpair[0]), (eS, c_exp_lastpair[1]),
                           (dS, c_p2_r1_last), (eV, c_wo_ev)]
                elif j > 0 and tt == 0:
                    wx += [(dS, c_ost_dma[(j - 1, 2)]),
                           (dS, c_ost_dma[(j - 1, 3)])]
                wxs.append(wx)
                wls.append([(eT, c_lnfree3[u])])
            sqs, norms = emit_ln2(
                [p3_r1[:, (pair * 2 + u) * D:(pair * 2 + u + 1) * D]
                 for u in range(2)],
                [ln2v[:, u * D:(u + 1) * D] for u in range(2)],
                wxs, wls, tag=f"p3_ln_{j}_{pair}")
            for u in range(2):
                tt = pair * 2 + u
                c_lnT_ready, c_tr = emit_transpose(
                    ln2v[:, u * D:(u + 1) * D], p3_lnT, tt, norms[u],
                    extra_first=[(eV, c_p3prev)] if tt == 0 else (),
                    tag=f"p3_{j}_{tt}")
                c_lnfree3[u] = c_tr

        for ic in range(IC):
            slot = ic % 2
            w = [(eT, c_p2_T)]
            kprev = (j, ic - 2) if ic >= 2 else ((j - 1, IC - 2 + ic) if j > 0 else None)
            if kprev is not None and kprev in c_gsil:
                w += [(eS, c_gsil[kprev]), (eV, c_hdn_v[kprev])]
            c_g = dma("gpsimd", p3_gu[:, slot * D:(slot + 1) * D], d_gate[ic],
                      waits=w, tag=f"p3_gld_{j}_{ic}")
            c_u = dma("gpsimd", p3_gu[:, (2 + slot) * D:(3 + slot) * D], d_up[ic],
                      tag=f"p3_uld_{j}_{ic}")
            c_gu = dG.n
            kp1 = (j, ic - 1) if ic >= 1 else ((j - 1, IC - 1) if j > 0 else None)
            for hc in range(HC):
                w = []
                if hc == 0:
                    w = [(eV, c_lnT_ready), (dG, c_g)]
                    if kp1 is not None and kp1 in c_gsil:
                        w.append((eS, c_gsil[kp1]))
                    if j == 0 and ic == 0:
                        w.append((eV, c_lz_free))
                ten(lambda e, hc=hc, slot=slot: e.matmul(
                    psB[:, 0:512],
                    p3_gu[:, slot * D + hc * 128:slot * D + (hc + 1) * 128],
                    p3_lnT[:, hc * 512:(hc + 1) * 512],
                    start=(hc == 0), stop=(hc == HC - 1)),
                    waits=w, inc=(hc == HC - 1), tag=f"p3_gmm_{j}_{ic}")
            c_gmm = eT.n
            for hc in range(HC):
                w = []
                if hc == 0:
                    w = [(dG, c_gu)]
                    if kp1 is not None and kp1 in c_hdn_v:
                        w.append((eV, c_hdn_v[kp1]))
                    if j == 0 and ic == 0:
                        w.append((eV, c_lz_free))
                ten(lambda e, hc=hc, slot=slot: e.matmul(
                    psB[:, 512:1024],
                    p3_gu[:, (2 + slot) * D + hc * 128:(2 + slot) * D + (hc + 1) * 128],
                    p3_lnT[:, hc * 512:(hc + 1) * 512],
                    start=(hc == 0), stop=(hc == HC - 1)),
                    waits=w, inc=(hc == HC - 1), tag=f"p3_umm_{j}_{ic}")
            c_umm = eT.n
            gslot = ic % 2
            w = [(eT, c_gmm), (dS, c_ds)]
            kprev2 = (j, ic - 2) if ic >= 2 else ((j - 1, IC - 2 + ic) if j > 0 else None)
            if kprev2 is not None and kprev2 in c_hdn_v:
                w.append((eV, c_hdn_v[kprev2]))
            if j == 0 and ic < 2:
                w.append((dS, c_p2_r1_last))
            c_gsil[(j, ic)] = act(lambda e, gslot=gslot, ic=ic: e.activation(
                out=p3_gub[:, gslot * 512:(gslot + 1) * 512],
                in_=psB[:, 0:512], func=AF.Silu, scale=t_gs[:, ic:ic + 1]),
                waits=w, tag=f"p3_silu_{j}_{ic}")
            c_hdn_v[(j, ic)] = vec(lambda e, gslot=gslot, ic=ic:
                                   e.scalar_tensor_tensor(
                p3_hdn[:, ic * 512:(ic + 1) * 512], psB[:, 512:1024],
                t_us[:, ic:ic + 1],
                p3_gub[:, gslot * 512:(gslot + 1) * 512],
                ALU.mult, ALU.mult),
                waits=[(eT, c_umm), (eS, c_gsil[(j, ic)])],
                tag=f"p3_hdn_{j}_{ic}")
        c_hdn_all = c_hdn_v[(j, IC - 1)]

        for g in range(n_groups):
            ics = list(range(g * DOWN_G, min((g + 1) * DOWN_G, IC)))
            slot = g % 2
            w = [(eT, c_p2_T)]
            kprev = (j, g - 2) if g >= 2 else ((j - 1, n_groups - 2 + g) if j > 0 else None)
            if kprev is not None and kprev in c_dn_freeT:
                w.append((eT, c_dn_freeT[kprev]))
            c_dn = dma(
                "gpsimd",
                p3_dn[:, slot * DOWN_G * D: slot * DOWN_G * D + len(ics) * D]
                .rearrange("p (i f) -> p i f", f=D),
                d_down[ics[0]:ics[0] + len(ics)].rearrange("i p f -> p i f"),
                waits=w, tag=f"p3_dnld_{j}_{g}")
            for qs in range(4):
                for hh in range(2):
                    hpar = (g * 8 + qs * 2 + hh) % 2
                    pd = psA[:, hpar * 1024:(hpar + 1) * 1024]
                    for ii, ic in enumerate(ics):
                        for dq in range(2):
                            w = []
                            if ii == 0 and dq == 0:
                                w = [(dG, c_dn), (eV, c_hdn_all),
                                     (eV, c_d_free[hpar])]
                                if j == 0 and g == 0 and qs == 0:
                                    w.append((eV, c_wo_ev))
                            ten(lambda e, ii=ii, ic=ic, qs=qs, dq=dq, hh=hh,
                                pd=pd, slot=slot, nics=len(ics): e.matmul(
                                pd[:, dq * 512:(dq + 1) * 512],
                                p3_hdn[:, ic * 512 + qs * 128:ic * 512 + qs * 128 + 128],
                                p3_dn[:, slot * DOWN_G * D + ii * D + hh * 1024 + dq * 512:
                                      slot * DOWN_G * D + ii * D + hh * 1024 + (dq + 1) * 512],
                                start=(ii == 0), stop=(ii == nics - 1)),
                                waits=w, inc=(ii == len(ics) - 1 and dq == 1),
                                tag=f"p3_dmm_{j}_{g}_{qs}_{hh}")
                    cdm = eT.n
                    dst = p3_acc[:, qs * D + hh * 1024: qs * D + (hh + 1) * 1024]
                    if g == 0:
                        c_d_free[hpar] = vec(
                            lambda e, dst=dst, pd=pd: e.tensor_copy(dst, pd),
                            waits=[(eT, cdm)], tag=f"p3_acc0_{j}_{g}_{qs}_{hh}")
                    else:
                        c_d_free[hpar] = vec(
                            lambda e, dst=dst, pd=pd: e.tensor_add(dst, dst, pd),
                            waits=[(eT, cdm)], tag=f"p3_acc_{j}_{g}_{qs}_{hh}")
                    c_acc_last = c_d_free[hpar]
            c_dn_freeT[(j, g)] = eT.n

        for qs in range(4):
            oslot = qs % 2
            ofs = oslot * D
            w = [(eV, c_acc_last), (dS, c_ds)]
            pk = (j, qs - 2) if qs >= 2 else ((j - 1, qs + 2) if j > 0 else None)
            if pk is not None and pk in c_ost_dma:
                w.append((dS, c_ost_dma[pk]))
            if j == 0 and qs < 2:
                w.append((eV, c_lnT_ready))
            cm = vec(lambda e, qs=qs, ofs=ofs: e.tensor_mul(
                p3_ost[:, ofs:ofs + D], p3_acc[:, qs * D:(qs + 1) * D], t_ds[:]),
                waits=w, tag=f"p3_mul_{j}_{qs}")
            ca = vec(lambda e, qs=qs, ofs=ofs: e.tensor_add(
                p3_ost[:, ofs:ofs + D], p3_ost[:, ofs:ofs + D],
                p3_r1[:, qs * D:(qs + 1) * D]),
                waits=[(eV, cm)], tag=f"p3_add_{j}_{qs}")
            t0 = j * 512 + qs * 128
            c_ost_dma[(j, qs)] = dma("sync", d_out[t0:t0 + 128, :],
                                     p3_ost[:, ofs:ofs + D],
                                     waits=[(eV, ca)], tag=f"p3_out_{j}_{qs}")
            c_out_last = c_ost_dma[(j, qs)]
        c_p3prev = eV.n

    if DEBUG:
        dbgs = [
            ("dbg_kt", s_kt, [NH, 128, T], BF16),
            ("dbg_v2", s_v2, [T, NH * 128], BF16),
            ("dbg_qt", s_qt, [NBLK, NH, 128, 512], BF16),
            ("dbg_r1", s_r1, [NBLK, 4, 128, D], BF16),
        ]
        for nm, src, shp, dt in dbgs:
            dd = nc.dram_tensor(nm, shp, dt, kind="ExternalOutput").ap()
            c_out_last = dma("sync", dd, src, waits=[(dS, c_out_last)],
                             tag=f"dbg_{nm}")
        for nm, src in (("dbg_acc", p3_acc), ("dbg_hdn", p3_hdn),
                        ("dbg_r1q", p3_r1), ("dbg_stat", t_stat),
                        ("dbg_mask", t_mask), ("dbg_lnT", p3_lnT)):
            ap = src[:]
            dd = nc.dram_tensor(nm, list(ap.shape), ap.dtype,
                                kind="ExternalOutput").ap()
            c_out_last = dma("sync", dd, ap, waits=[(dS, c_out_last)],
                             tag=f"dbg_{nm}")

    P.op("sync", lambda e: e.nop(), waits=[(dS, c_out_last)], tag="final")

    P.check()

    from contextlib import ExitStack
    ctx = ExitStack()
    for s in sems:
        s.h = ctx.enter_context(nc.semaphore(s.name))
    _emit_all(nc, P, sems)
    ctx.close()
    return nc


def _emit_all(nc, P, sems):
    dS, dG, eT, eV, eS = sems

    def run_engine(eng_obj, name):
        self_sem = {"vector": eV, "scalar": eS}.get(name)
        cnt = 0
        for fn, waits, incs, tag in P.ops[name]:
            if self_sem is not None and cnt > 0:
                eng_obj.wait_ge(self_sem.h, cnt)
            for s, v in waits:
                eng_obj.wait_ge(s.h, v)
            inst = fn(eng_obj)
            for s, amt in incs:
                inst.then_inc(s.h, amt)
                if s is self_sem:
                    cnt += amt

    with nc.Block() as block:
        @block.sync
        def _(e):
            run_engine(e, "sync")

        @block.scalar
        def _(e):
            run_engine(e, "scalar")

        @block.vector
        def _(e):
            run_engine(e, "vector")

        @block.tensor
        def _(e):
            run_engine(e, "tensor")

        @block.gpsimd
        def _(e):
            run_engine(e, "gpsimd")


def _host_masks(role):
    k = np.arange(128)[:, None]
    q = np.arange(512)[None, :]
    m = np.zeros((128, 8 * 512), np.float32)
    for u in range(8):
        m[:, u * 512:(u + 1) * 512] = (q + 512 * role - 128 * u - k >= 0)
    return m


_NC_CACHE = [None]


def kernel(L0_attn_x, L0_attn_Wq, L0_attn_Wk, L0_attn_Wv, L0_attn_Wo,
           L0_ffn_gate_w, L0_ffn_up_w, L0_ffn_down_w, L0_ffn_gate_s,
           L0_ffn_up_s, L0_ffn_down_s):
    x = np.ascontiguousarray(np.asarray(L0_attn_x, dtype=np.float32))
    Wq = np.asarray(L0_attn_Wq, np.float32)
    Wk = np.asarray(L0_attn_Wk, np.float32)
    Wv = np.asarray(L0_attn_Wv, np.float32)
    Wo = np.asarray(L0_attn_Wo, np.float32)
    gw = np.asarray(L0_ffn_gate_w, np.float32)
    uw = np.asarray(L0_ffn_up_w, np.float32)
    dw = np.asarray(L0_ffn_down_w, np.float32)
    gs = np.asarray(L0_ffn_gate_s, np.float32)
    us = np.asarray(L0_ffn_up_s, np.float32)
    ds = np.asarray(L0_ffn_down_s, np.float32)

    def qk_img(W):
        A = W.reshape(HC, 128, HC, 128)
        return np.ascontiguousarray(A.transpose(3, 0, 2, 1)).reshape(128, -1)

    def vo_img(W):
        A = W.reshape(D, HC, 128)
        return np.ascontiguousarray(A.transpose(2, 1, 0)).reshape(128, -1)

    def gu_img(W):
        A = W.reshape(IC, 128, HC, 128)
        return np.ascontiguousarray(A.transpose(0, 3, 2, 1)).reshape(IC, 128, -1)

    def dn_img(W):
        A = W.reshape(D, IC, 128)
        return np.ascontiguousarray(A.transpose(1, 2, 0))

    wk_i = qk_img(Wk)
    wq_i = qk_img(Wq)
    wv_i = vo_img(Wv)
    wo_i = vo_img(Wo)
    g_i = gu_img(gw)
    u_i = gu_img(uw)
    d_i = dn_img(dw)
    gs_i = np.ascontiguousarray(gs.reshape(IC, 128).T)
    us_i = np.ascontiguousarray(us.reshape(IC, 128).T)
    ds_i = np.ascontiguousarray(np.broadcast_to(ds[None, :], (128, D)))
    ident = np.eye(128, dtype=np.float32)
    masks = [_host_masks(0), _host_masks(1)]

    in_maps = []
    for c in range(8):
        b, r = c // 2, c % 2
        rows = np.concatenate(
            [np.arange(512 * (2 * jj + r), 512 * (2 * jj + r) + 512)
             for jj in range(NBLK)])
        in_maps.append({
            "xkv": x[b],
            "xq": np.ascontiguousarray(x[b][rows]),
            "wk": wk_i, "wq": wq_i, "wv": wv_i, "wo": wo_i,
            "gate": g_i, "up": u_i, "down": d_i,
            "gs": gs_i, "us": us_i, "ds": ds_i,
            "maskin": masks[r], "identin": ident,
        })

    if _NC_CACHE[0] is None:
        _NC_CACHE[0] = build_program()
    nc = _NC_CACHE[0]
    res = run_bass_kernel_spmd(nc, in_maps, core_ids=list(range(8)))
    if DEBUG:
        _NC_CACHE.append(res)

    out = np.empty((B, T, D), np.float32)
    for c in range(8):
        b, r = c // 2, c % 2
        oc = res.results[c]["out"]
        for jj in range(NBLK):
            out[b, 512 * (2 * jj + r):512 * (2 * jj + r) + 512] = \
                oc[512 * jj:512 * (jj + 1)]
    return out
